# version 10 (final)
# speedup vs baseline: 1.0902x; 1.0031x over previous
import math
from contextlib import ExitStack

import numpy as np
import concourse.bass as bass
import concourse.mybir as mybir
from concourse.bass_utils import run_bass_kernel_spmd

F32 = mybir.dt.float32
BF16 = mybir.dt.bfloat16
AF = mybir.ActivationFunctionType
ALU = mybir.AluOpType

NCORES = 8
PI = float(np.pi)


class Prog:
    ENG = ("sync", "tensor", "vector", "scalar", "gpsimd")

    def __init__(self):
        self.q = {e: [] for e in self.ENG}
        self.cnt = {}
        self.ambient = ()

    def op(self, eng, fn, waits=(), inc=None, amt=1, amb=False):
        waits = tuple(w for w in waits if w is not None)
        if amb and self.ambient:
            waits = waits + tuple(self.ambient)
            d = {}
            for s, v in waits:
                d[s] = max(d.get(s, 0), v)
            waits = tuple(d.items())
        ev = None
        if inc is not None:
            v = self.cnt.get(inc, 0) + amt
            self.cnt[inc] = v
            ev = (inc, v)
        self.q[eng].append((fn, waits, inc, amt))
        return ev

    def emit(self, nc, block, stack):
        sems = {n: stack.enter_context(nc.semaphore(n)) for n in self.cnt}
        for eng in self.ENG:
            items = self.q[eng]
            if not items:
                continue

            def body(E, items=items):
                for fn, waits, inc, amt in items:
                    for s, v in waits:
                        E.wait_ge(sems[s], v)
                    inst = fn(E)
                    if inc is not None:
                        inst.then_inc(sems[inc], amt)

            getattr(block, eng)(body)


def after(*evs):
    out = []
    for e in evs:
        if e is None:
            continue
        if isinstance(e, tuple) and len(e) == 2 and isinstance(e[0], str):
            out.append(e)
        else:
            out.extend(x for x in e if x is not None)
    d = {}
    for s, v in out:
        d[s] = max(d.get(s, 0), v)
    return [(s, v) for s, v in d.items()]


def ld_bf16(p, sbuf_ap, dram_ap, waits=(), q="gpsimd"):
    sem = {"gpsimd": "ldg", "scalar": "lda", "sync": "lds"}[q]
    return p.op(q, lambda E: E.dma_start(out=sbuf_ap, in_=dram_ap),
                waits=waits, inc=sem, amt=16, amb=True)


def ld_f32(p, sbuf_ap, dram_ap, waits=(), q="sync"):
    sem = {"sync": "lds", "scalar": "lda"}[q]
    return p.op(q, lambda E: E.dma_start(out=sbuf_ap, in_=dram_ap),
                waits=waits, inc=sem, amt=16, amb=True)


def st(p, dram_ap, sbuf_ap, waits=(), q="sync"):
    return p.op(q, lambda E: E.dma_start(out=dram_ap, in_=sbuf_ap),
                waits=waits, inc="st", amt=16, amb=True)


def mm_group(p, psum_ap, pairs, waits=()):
    n = len(pairs)
    ev = None
    for i, (lt, rh) in enumerate(pairs):
        ev = p.op("tensor",
                  (lambda E, lt=lt, rh=rh, i=i, n=n: E.matmul(
                      psum_ap, lt, rh, start=(i == 0), stop=(i == n - 1))),
                  waits=(waits if i == 0 else ()),
                  inc=("pe" if i == n - 1 else None))
    return ev


def vop(p, fn, waits=()):
    return p.op("vector", fn, waits=waits, inc="dv")


def gop(p, fn, waits=()):
    return p.op("gpsimd", fn, waits=waits, inc="gp", amb=True)


def aop(p, fn, waits=()):
    return p.op("scalar", fn, waits=waits, inc="ac")


def fence(p, eng, waits):
    p.op(eng, lambda E: None, waits=waits)


def emit_ln_stats(p, ones_col_bf, y_tiles_bf, sq_tiles_bf, ps_sum, ps_sq,
                  rowsl, rowA, rowB, eps, waits, d_model=1024):
    ev_sum = mm_group(p, ps_sum[0:1, :], [(ones_col_bf, t) for t in y_tiles_bf], waits=waits)
    ev_sq = mm_group(p, ps_sq[0:1, :], [(ones_col_bf, t) for t in sq_tiles_bf], waits=waits)
    mu, e2, t0 = rowsl(0), rowsl(1), rowsl(2)
    e1 = vop(p, lambda E: E.tensor_scalar_mul(mu, ps_sum[0:1, :], 1.0 / d_model), waits=after(ev_sum))
    e2e = vop(p, lambda E: E.tensor_scalar_mul(e2, ps_sq[0:1, :], 1.0 / d_model), waits=after(ev_sq))
    e3 = vop(p, lambda E: E.tensor_tensor(t0, mu, mu, ALU.mult), waits=after(e1))
    e4 = vop(p, lambda E: E.tensor_tensor(t0, e2, t0, ALU.subtract), waits=after(e2e, e3))
    e4b = vop(p, lambda E: E.tensor_scalar_add(t0, t0, eps), waits=after(e4))
    e5 = aop(p, lambda E: E.activation(t0, t0, AF.Sqrt), waits=after(e4b))
    e6 = vop(p, lambda E: E.reciprocal(rowA, t0), waits=after(e5))
    e7 = vop(p, lambda E: E.tensor_tensor(rowB, mu, rowA, ALU.mult), waits=after(e6))
    return after(e6, e7)


def emit_bcast_row(p, ps_out, ones_col128_f32, row_f32, waits):
    return mm_group(p, ps_out, [(ones_col128_f32, row_f32)], waits=waits)


def emit_s1a(nc, p, stack_outer, dio, B, T, ev0=(), pfx=''):
    TOK = B * T
    CH = 512
    CPB = T // CH
    assert T % CH == 0
    xT = dio["xT"]
    b_in = dio["b_in"]
    w_mid = dio["w_mid"]
    mid = dio["mid"]

    w_in = dio["w_in"]
    pl = dio["pl"]
    b_mid = dio["b_mid"]

    with ExitStack() as stack:
        e = stack.enter_context
        sb = lambda n, sh, dt=F32: e(nc.sbuf_tensor(pfx + n, sh, dt))[:]
        w0 = sb("w0", [128, 8192], BF16)
        w1i = sb("w1i", [128, 8192], BF16)
        w2o = sb("w2o", [128, 8192], BF16)
        wm0 = sb("wm0", [128, 8192], BF16)
        wm1n = sb("wm1n", [128, 8192], BF16)
        binb = sb("binb", [128, 24], F32)
        plb = sb("plb", [128, 24], F32)
        bmidb = sb("bmidb", [128, 16], F32)
        bmid = sb("bmid", [128, 8], F32)
        pr = sb("pr", [128, 176], F32)
        Pre = sb("Pre", [128, 4096], BF16)
        Pim = sb("Pim", [128, 4096], BF16)
        Qre = sb("Qre", [128, 4096], BF16)
        Qim = sb("Qim", [128, 4096], BF16)
        tbl = sb("tbl", [128, 1024], F32)
        tb1 = sb("tb1", [128, 512], F32)
        tb2 = sb("tb2", [128, 512], F32)
        xbf = sb("xbf", [128, 4096], BF16)
        uo = sb("uo", [128, 4096], F32)
        tre = sb("tre", [128, 512], BF16)
        tim = sb("tim", [128, 512], BF16)
        vre = sb("vre", [128, 512], BF16)
        vim = sb("vim", [128, 512], BF16)
        sre = sb("sre", [128, 512], BF16)
        sim = sb("sim", [128, 512], BF16)
        wk = sb("wk", [128, 1024], BF16)
        hre = sb("hre", [128, 4096], BF16)
        him = sb("him", [128, 4096], BF16)
        outb = sb("outb", [128, 4096], F32)
        init_re = sb("init_re", [128, 8], F32)
        init_im = sb("init_im", [128, 8], F32)
        onesb = sb("onesb", [128, 512], BF16)
        ps = [e(nc.psum_tensor(pfx + f"ps{i}", [128, 512], F32))[:] for i in range(6)]

        lw = []
        for k in range(8):
            lw.append(ld_bf16(p, w0[:, k * 1024:(k + 1) * 1024], w_in[0, k * 128:(k + 1) * 128, :]))
            lw.append(ld_bf16(p, w1i[:, k * 1024:(k + 1) * 1024], w_in[1, k * 128:(k + 1) * 128, :]))
            lw.append(ld_bf16(p, w2o[:, k * 1024:(k + 1) * 1024], w_in[2, k * 128:(k + 1) * 128, :]))
            lw.append(ld_bf16(p, wm0[:, k * 1024:(k + 1) * 1024], w_mid[0, k * 128:(k + 1) * 128, :]))
        lp = [ld_f32(p, binb, b_in), ld_f32(p, plb, pl), ld_f32(p, bmidb, b_mid)]
        ev_neg = []
        prev_neg = None
        for k in range(8):
            lk = ld_f32(p, tbl[:, 0:1024], w_mid[1, k * 128:(k + 1) * 128, :], waits=after(prev_neg))
            prev_neg = vop(p, lambda E, k=k: E.tensor_scalar_mul(
                wm1n[:, k * 1024:(k + 1) * 1024], tbl[:, 0:1024], -1.0),
                waits=after(lk))
            ev_neg.append(prev_neg)
        ev_bmid = vop(p, lambda E: E.tensor_tensor(bmid, bmidb[:, 0:8], bmidb[:, 8:16], ALU.subtract),
                      waits=after(*lp))
        ev_ones = gop(p, lambda E: E.memset(onesb, 1.0))

        NU, TH, GA = 0, 8, 16
        MSK, THR, SH, CT, ST_ = 24, 32, 40, 48, 56
        ENU, ENUN, WRE, WIM, LRE, LIM = 64, 72, 80, 88, 96, 104
        PA, PB = 112, 120
        SC1, SC2, SC3, SC4 = 128, 136, 144, 152
        c8 = lambda o: pr[:, o:o + 8]
        ev = aop(p, lambda E: E.activation(pr[:, 0:24], plb, AF.Exp), waits=after(*lp))
        ev1 = vop(p, lambda E: E.tensor_single_scalar(c8(MSK), c8(TH), PI, ALU.is_gt), waits=after(ev))
        ev2 = vop(p, lambda E: E.scalar_tensor_tensor(c8(THR), c8(MSK), -2.0 * PI, c8(TH), ALU.mult, ALU.add),
                  waits=after(ev1))
        ev3 = aop(p, lambda E: E.activation(c8(ST_), c8(THR), AF.Sin), waits=after(ev2))
        ev4 = aop(p, lambda E: E.activation(c8(SH), c8(THR), AF.Sin, scale=0.5), waits=after(ev2))
        ev5 = vop(p, lambda E: E.tensor_tensor(c8(CT), c8(SH), c8(SH), ALU.mult), waits=after(ev4))
        ev6 = vop(p, lambda E: E.tensor_scalar(c8(CT), c8(CT), -2.0, 1.0, ALU.mult, ALU.add), waits=after(ev5))
        ev7 = aop(p, lambda E: E.activation(c8(ENU), c8(NU), AF.Exp), waits=after(ev))
        ev8 = aop(p, lambda E: E.activation(c8(ENUN), c8(NU), AF.Exp, scale=-1.0), waits=after(ev))
        ev9 = vop(p, lambda E: E.tensor_tensor(c8(WRE), c8(ENU), c8(CT), ALU.mult), waits=after(ev6, ev7))
        ev10 = vop(p, lambda E: E.tensor_tensor(c8(SC1), c8(ENU), c8(ST_), ALU.mult), waits=after(ev3, ev7))
        ev11 = vop(p, lambda E: E.tensor_scalar_mul(c8(WIM), c8(SC1), -1.0), waits=after(ev10))
        ev12 = vop(p, lambda E: E.tensor_tensor(c8(LRE), c8(ENUN), c8(CT), ALU.mult), waits=after(ev6, ev8))
        ev13 = vop(p, lambda E: E.tensor_tensor(c8(LIM), c8(ENUN), c8(ST_), ALU.mult), waits=after(ev3, ev8))
        ev_pq = after(ev9, ev11, ev12, ev13)

        def build_tab(Tre_t, Tim_t, base_re, base_im, g0, ev_start):
            ev_all = []
            for m in range(8):
                tr = tbl[:, 0:512]
                ti = tbl[:, 512:1024]
                if g0 is None:
                    e0 = vop(p, lambda E, tr=tr: E.memset(tr[:, 0:1], 1.0), waits=ev_start)
                else:
                    e0 = vop(p, lambda E, tr=tr, m=m: E.tensor_copy(tr[:, 0:1], pr[:, g0 + m:g0 + m + 1]),
                             waits=ev_start)
                e0b = vop(p, lambda E, ti=ti: E.memset(ti[:, 0:1], 0.0), waits=ev_start)
                eA = vop(p, lambda E, m=m: E.tensor_copy(pr[:, PA:PA + 1], pr[:, base_re + m:base_re + m + 1]),
                         waits=ev_start)
                eB = vop(p, lambda E, m=m: E.tensor_copy(pr[:, PB:PB + 1], pr[:, base_im + m:base_im + m + 1]),
                         waits=ev_start)
                prev = after(e0, e0b, eA, eB)
                k = 1
                while k < 512:
                    a = pr[:, PA:PA + 1]
                    b = pr[:, PB:PB + 1]
                    ee1 = vop(p, lambda E, k=k, a=a, tr=tr: E.tensor_scalar_mul(tb1[:, 0:k], tr[:, 0:k], a), waits=prev)
                    ee2 = vop(p, lambda E, k=k, b=b, ti=ti: E.tensor_scalar_mul(tb2[:, 0:k], ti[:, 0:k], b), waits=prev)
                    ee3 = vop(p, lambda E, k=k, tr=tr: E.tensor_tensor(tr[:, k:2 * k], tb1[:, 0:k], tb2[:, 0:k], ALU.subtract),
                              waits=after(ee1, ee2))
                    ee4 = vop(p, lambda E, k=k, b=b, tr=tr: E.tensor_scalar_mul(tb1[:, 0:k], tr[:, 0:k], b), waits=after(ee3))
                    ee5 = vop(p, lambda E, k=k, a=a, ti=ti: E.tensor_scalar_mul(tb2[:, 0:k], ti[:, 0:k], a), waits=prev)
                    ee6 = vop(p, lambda E, k=k, ti=ti: E.tensor_tensor(ti[:, k:2 * k], tb1[:, 0:k], tb2[:, 0:k], ALU.add),
                              waits=after(ee4, ee5))
                    if 2 * k < 512:
                        ee7 = vop(p, lambda E, a=a: E.tensor_tensor(pr[:, SC2:SC2 + 1], a, a, ALU.mult), waits=after(ee4, ee5))
                        ee8 = vop(p, lambda E, b=b: E.tensor_tensor(pr[:, SC3:SC3 + 1], b, b, ALU.mult), waits=after(ee4, ee5))
                        ee9 = vop(p, lambda E, a=a, b=b: E.tensor_tensor(pr[:, SC4:SC4 + 1], a, b, ALU.mult), waits=after(ee4, ee5))
                        ee10 = vop(p, lambda E, a=a: E.tensor_tensor(a, pr[:, SC2:SC2 + 1], pr[:, SC3:SC3 + 1], ALU.subtract),
                                   waits=after(ee7, ee8, ee9))
                        ee11 = vop(p, lambda E, b=b: E.tensor_scalar_mul(b, pr[:, SC4:SC4 + 1], 2.0), waits=after(ee9, ee10))
                        prev = after(ee6, ee10, ee11, ee3)
                    else:
                        prev = after(ee3, ee6)
                    k *= 2
                ec1 = vop(p, lambda E, m=m, tr=tr: E.tensor_copy(Tre_t[:, m * 512:(m + 1) * 512], tr), waits=prev)
                ec2 = vop(p, lambda E, m=m, ti=ti: E.tensor_copy(Tim_t[:, m * 512:(m + 1) * 512], ti), waits=prev)
                ev_all.append(after(ec1, ec2))
            return after(*ev_all)

        ev_P = build_tab(Pre, Pim, WRE, WIM, GA, ev_pq)
        ev_Q = build_tab(Qre, Qim, LRE, LIM, None, ev_pq)
        ev_tabs = after(ev_P, ev_Q)

        ev_w = after(*lw)
        ev_wm1 = after(*ev_neg)
        bank_free = [ev0] * 6
        ev_chunk = ev0
        ev_out_st = None
        ev_tre_free = ev0
        ev_tim_free = None

        for b in range(B):
            ez1 = vop(p, lambda E: E.memset(init_re, 0.0), waits=after(ev_chunk))
            ez2 = vop(p, lambda E: E.memset(init_im, 0.0), waits=after(ez1))
            ev_init = after(ez1, ez2)
            for tc in range(CPB):
                c = b * CPB + tc
                lx = []
                for k in range(8):
                    lx.append(ld_bf16(p, xbf[:, k * 512:(k + 1) * 512],
                                      xT[k * 128:(k + 1) * 128, c * 512:(c + 1) * 512],
                                      waits=after(ev_chunk)))
                ev_lx = after(*lx)
                ev_h_all = []
                for m in range(8):
                    b0, b1, b2 = (m % 2) * 3, (m % 2) * 3 + 1, (m % 2) * 3 + 2
                    g1 = mm_group(p, ps[b0], [(w0[:, k * 1024 + m * 128: k * 1024 + (m + 1) * 128],
                                               xbf[:, k * 512:(k + 1) * 512]) for k in range(8)],
                                  waits=after(ev_lx, ev_w, bank_free[b0]))
                    g2 = mm_group(p, ps[b1], [(w1i[:, k * 1024 + m * 128: k * 1024 + (m + 1) * 128],
                                               xbf[:, k * 512:(k + 1) * 512]) for k in range(8)],
                                  waits=after(ev_lx, ev_w, bank_free[b1]))
                    g3 = mm_group(p, ps[b2], [(w2o[:, k * 1024 + m * 128: k * 1024 + (m + 1) * 128],
                                               xbf[:, k * 512:(k + 1) * 512]) for k in range(8)],
                                  waits=after(ev_lx, ev_w, bank_free[b2]))
                    e_r = aop(p, lambda E, b0=b0, m=m: E.activation(tre, ps[b0], AF.Identity, bias=binb[:, m:m + 1]),
                              waits=after(g1, ev_tre_free, *lp))
                    e_i = aop(p, lambda E, b1=b1, m=m: E.activation(tim, ps[b1], AF.Identity, bias=binb[:, 8 + m:8 + m + 1]),
                              waits=after(g2, ev_tim_free, *lp))
                    e_o = aop(p, lambda E, b2=b2, m=m: E.activation(uo[:, m * 512:(m + 1) * 512], ps[b2], AF.Identity,
                                                                    bias=binb[:, 16 + m:16 + m + 1]),
                              waits=after(g3, *lp))
                    bank_free[b0], bank_free[b1], bank_free[b2] = e_r, e_i, e_o
                    Pr = Pre[:, m * 512:(m + 1) * 512]
                    Pi = Pim[:, m * 512:(m + 1) * 512]
                    x1 = vop(p, lambda E, Pr=Pr: E.tensor_tensor(wk[:, 0:512], tre, Pr, ALU.mult), waits=after(e_r, ev_tabs))
                    x2 = vop(p, lambda E, Pi=Pi: E.tensor_tensor(wk[:, 512:1024], tim, Pi, ALU.mult), waits=after(e_i))
                    x3 = vop(p, lambda E: E.tensor_tensor(vre, wk[:, 0:512], wk[:, 512:1024], ALU.subtract), waits=after(x1, x2))
                    x4 = vop(p, lambda E, Pr=Pr: E.tensor_tensor(wk[:, 0:512], tim, Pr, ALU.mult), waits=after(x3))
                    x5 = vop(p, lambda E, Pi=Pi: E.tensor_tensor(wk[:, 512:1024], tre, Pi, ALU.mult), waits=after(x3))
                    x6 = vop(p, lambda E: E.tensor_tensor(vim, wk[:, 0:512], wk[:, 512:1024], ALU.add), waits=after(x4, x5))
                    ev_tre_free, ev_tim_free = x5, x4
                    if tc == 0:
                        s1 = vop(p, lambda E: E.tensor_tensor_scan(sre, onesb, vre, 0.0, ALU.mult, ALU.add),
                                 waits=after(x3, ev_ones))
                        s2 = vop(p, lambda E: E.tensor_tensor_scan(sim, onesb, vim, 0.0, ALU.mult, ALU.add),
                                 waits=after(x6, ev_ones))
                    else:
                        s1 = vop(p, lambda E, m=m: E.tensor_tensor_scan(sre, onesb, vre, init_re[:, m:m + 1], ALU.mult, ALU.add),
                                 waits=after(x3, ev_ones, ev_init))
                        s2 = vop(p, lambda E, m=m: E.tensor_tensor_scan(sim, onesb, vim, init_im[:, m:m + 1], ALU.mult, ALU.add),
                                 waits=after(x6, ev_ones, ev_init))
                    Qr = Qre[:, m * 512:(m + 1) * 512]
                    Qi = Qim[:, m * 512:(m + 1) * 512]
                    hr = hre[:, m * 512:(m + 1) * 512]
                    hi = him[:, m * 512:(m + 1) * 512]
                    y1_ = vop(p, lambda E, Qr=Qr: E.tensor_tensor(wk[:, 0:512], sre, Qr, ALU.mult), waits=after(s1))
                    y2_ = vop(p, lambda E, Qi=Qi: E.tensor_tensor(wk[:, 512:1024], sim, Qi, ALU.mult), waits=after(s2))
                    y3_ = vop(p, lambda E, hr=hr: E.tensor_tensor(hr, wk[:, 0:512], wk[:, 512:1024], ALU.subtract),
                              waits=after(y1_, y2_))
                    y4_ = vop(p, lambda E, Qi=Qi: E.tensor_tensor(wk[:, 0:512], sre, Qi, ALU.mult), waits=after(y3_))
                    y5_ = vop(p, lambda E, Qr=Qr: E.tensor_tensor(wk[:, 512:1024], sim, Qr, ALU.mult), waits=after(y3_))
                    y6_ = vop(p, lambda E, hi=hi: E.tensor_tensor(hi, wk[:, 0:512], wk[:, 512:1024], ALU.add),
                              waits=after(y4_, y5_))
                    ev_h = after(y3_, y6_)
                    if tc != CPB - 1:
                        hcr = hr[:, 511:512]
                        hci = hi[:, 511:512]
                        z1 = vop(p, lambda E, hcr=hcr, m=m: E.tensor_tensor(pr[:, SC2:SC2 + 1], hcr, pr[:, LRE + m:LRE + m + 1], ALU.mult), waits=ev_h)
                        z2 = vop(p, lambda E, hci=hci, m=m: E.tensor_tensor(pr[:, SC3:SC3 + 1], hci, pr[:, LIM + m:LIM + m + 1], ALU.mult), waits=ev_h)
                        z3 = vop(p, lambda E, m=m: E.tensor_tensor(init_re[:, m:m + 1], pr[:, SC2:SC2 + 1], pr[:, SC3:SC3 + 1], ALU.subtract), waits=after(z1, z2))
                        z4 = vop(p, lambda E, hci=hci, m=m: E.tensor_tensor(pr[:, SC2:SC2 + 1], hci, pr[:, LRE + m:LRE + m + 1], ALU.mult), waits=after(z3))
                        z5 = vop(p, lambda E, hcr=hcr, m=m: E.tensor_tensor(pr[:, SC3:SC3 + 1], hcr, pr[:, LIM + m:LIM + m + 1], ALU.mult), waits=after(z3))
                        z6 = vop(p, lambda E, m=m: E.tensor_tensor(init_im[:, m:m + 1], pr[:, SC2:SC2 + 1], pr[:, SC3:SC3 + 1], ALU.add), waits=after(z4, z5))
                        ev_init = after(z3, z6)
                    ev_h_all.append(ev_h)
                ev_hall = after(*ev_h_all)
                ev_ev = []
                for mp in range(8):
                    bank = mp % 4
                    pairs = [(wm0[:, k * 1024 + mp * 128: k * 1024 + (mp + 1) * 128],
                              hre[:, k * 512:(k + 1) * 512]) for k in range(8)]
                    pairs += [(wm1n[:, k * 1024 + mp * 128: k * 1024 + (mp + 1) * 128],
                               him[:, k * 512:(k + 1) * 512]) for k in range(8)]
                    gmid = mm_group(p, ps[bank], pairs, waits=after(ev_hall, ev_wm1, ev_w, bank_free[bank]))
                    eo = vop(p, lambda E, bank=bank, mp=mp: E.scalar_tensor_tensor(
                        outb[:, mp * 512:(mp + 1) * 512], ps[bank], bmid[:, mp:mp + 1],
                        uo[:, mp * 512:(mp + 1) * 512], ALU.add, ALU.add),
                        waits=after(gmid, ev_bmid, ev_out_st))
                    bank_free[bank] = eo
                    ev_ev.append(eo)
                sts = [st(p, mid[mp * 128:(mp + 1) * 128, c * 512:(c + 1) * 512],
                          outb[:, mp * 512:(mp + 1) * 512], waits=after(ev_ev[mp])) for mp in range(8)]
                ev_out_st = after(*sts)
                ev_chunk = after(ev_out_st, *ev_ev)

        ev_ret = after(ev_out_st, ev_chunk)
    return ev_ret


def emit_s1b(nc, p, stack_outer, dio, B, T, ev0=(), pfx=''):
    TOK = B * T
    NCH = TOK // 512
    xT = dio["xT"]
    w1w = dio["w1w"]
    vecs = dio["vecs"]

    midT = dio["midT"]
    w2w = dio["w2w"]
    y1 = dio["y1"]

    with ExitStack() as stack:
        e = stack.enter_context
        sb = lambda n, sh, dt=F32: e(nc.sbuf_tensor(pfx + n, sh, dt))[:]
        w1b_ = sb("w1b_", [128, 8192], BF16)
        w2b_ = sb("w2b_", [128, 8192], BF16)
        vb = sb("vb", [128, 32], F32)
        mbf_ = [sb("mbf", [128, 4096], BF16), sb("mbf2", [128, 4096], BF16)]
        mf_ = [sb("mf", [128, 4096], F32), sb("mf2", [128, 4096], F32)]
        xf_ = [sb("xf", [128, 4096], F32), sb("xf2", [128, 4096], F32)]
        gb = sb("gb", [128, 4096], BF16)
        y2f = sb("y2f", [128, 4096], F32)
        sqb = sb("sqb", [128, 4096], BF16)
        ybf = sb("ybf", [128, 4096], BF16)
        sf = sb("sf", [128, 4096], F32)
        y1f = sb("y1f", [128, 4096], F32)
        rowA = sb("rowA", [1, 512], F32)
        rowB = sb("rowB", [1, 512], F32)
        rowA2 = sb("rowA2", [1, 512], F32)
        rowB2 = sb("rowB2", [1, 512], F32)
        rws = sb("rws", [1, 3072], F32)
        onec = sb("onec", [128, 1], BF16)
        onec32 = sb("onec32", [1, 128], F32)
        ps = [e(nc.psum_tensor(pfx + f"ps{i}", [128, 512], F32))[:] for i in range(4)]
        psS = e(nc.psum_tensor(pfx + "psS", [128, 512], F32))[:]
        psQ = e(nc.psum_tensor(pfx + "psQ", [128, 512], F32))[:]
        psA = e(nc.psum_tensor(pfx + "psA", [128, 512], F32))[:]
        psB = e(nc.psum_tensor(pfx + "psB", [128, 512], F32))[:]

        lw = []
        for k in range(8):
            lw.append(ld_bf16(p, w1b_[:, k * 1024:(k + 1) * 1024], w1w[k * 128:(k + 1) * 128, :]))
            lw.append(ld_bf16(p, w2b_[:, k * 1024:(k + 1) * 1024], w2w[k * 128:(k + 1) * 128, :]))
        lv = ld_f32(p, vb, vecs)
        e_one = gop(p, lambda E: E.memset(onec, 1.0))
        e_one32 = gop(p, lambda E: E.memset(onec32, 1.0))
        ev_w = after(*lw)

        bank_free = [ev0] * 4
        ev_stats_free = ev0
        ev_AB_free = None
        ev_chunk = ev0
        ev_st = None
        ev_mbf_rd = [ev0, ev0]
        ev_mf_rd = [ev0, ev0]
        ev_xf_rd = [ev0, ev0]
        for c in range(NCH):
            par = c % 2
            mbf, mf, xf = mbf_[par], mf_[par], xf_[par]
            lm, lmf, lxf = [], [], []
            for k in range(8):
                lm.append(ld_bf16(p, mbf[:, k * 512:(k + 1) * 512],
                                  midT[k * 128:(k + 1) * 128, c * 512:(c + 1) * 512], waits=after(ev_mbf_rd[par])))
                lmf.append(ld_f32(p, mf[:, k * 512:(k + 1) * 512],
                                  midT[k * 128:(k + 1) * 128, c * 512:(c + 1) * 512], waits=after(ev_mf_rd[par])))
                lxf.append(ld_f32(p, xf[:, k * 512:(k + 1) * 512],
                                  xT[k * 128:(k + 1) * 128, c * 512:(c + 1) * 512], waits=after(ev_xf_rd[par]), q="scalar"))
            ev_lm = after(*lm)
            ev_g = []
            ev_g1s = []
            for m in range(8):
                bank = m % 4
                g1 = mm_group(p, ps[bank], [(w1b_[:, k * 1024 + m * 128: k * 1024 + (m + 1) * 128],
                                             mbf[:, k * 512:(k + 1) * 512]) for k in range(8)],
                              waits=after(ev_lm, ev_w, bank_free[bank]))
                ev_g1s.append(g1)
                eg = aop(p, lambda E, bank=bank, m=m: E.activation(
                    gb[:, m * 512:(m + 1) * 512], ps[bank], AF.Gelu, bias=vb[:, m:m + 1]),
                    waits=after(g1, lv))
                bank_free[bank] = eg
                ev_g.append(eg)
            ev_gall = after(*ev_g)
            ev_mbf_rd[par] = after(*ev_g1s)
            ev_y = []
            for m in range(8):
                bank = m % 4
                g2 = mm_group(p, ps[bank], [(w2b_[:, k * 1024 + m * 128: k * 1024 + (m + 1) * 128],
                                             gb[:, k * 512:(k + 1) * 512]) for k in range(8)],
                              waits=after(ev_gall, bank_free[bank]))
                ey = vop(p, lambda E, bank=bank, m=m, mf=mf: E.scalar_tensor_tensor(
                    y2f[:, m * 512:(m + 1) * 512], ps[bank], vb[:, 8 + m:8 + m + 1],
                    mf[:, m * 512:(m + 1) * 512], ALU.add, ALU.add),
                    waits=after(g2, lv, *lmf))
                bank_free[bank] = ey
                ev_y.append(ey)
            ev_mf_rd[par] = after(*ev_y)
            sq_ev, cp_ev = [], []
            for m in range(8):
                sq_ev.append(aop(p, lambda E, m=m: E.activation(sqb[:, m * 512:(m + 1) * 512],
                                                                y2f[:, m * 512:(m + 1) * 512], AF.Square),
                                 waits=after(ev_y[m])))
                cp_ev.append(vop(p, lambda E, m=m: E.tensor_copy(ybf[:, m * 512:(m + 1) * 512],
                                                                 y2f[:, m * 512:(m + 1) * 512]),
                                 waits=after(ev_y[m])))
            ev_rows = emit_ln_stats(p, onec,
                                    [ybf[:, m * 512:(m + 1) * 512] for m in range(8)],
                                    [sqb[:, m * 512:(m + 1) * 512] for m in range(8)],
                                    psS, psQ, lambda i: rws[0:1, i * 512:(i + 1) * 512],
                                    rowA, rowB, 1e-5,
                                    after(after(*sq_ev), after(*cp_ev), e_one, ev_stats_free))
            ev_stats_free = ev_rows
            eA = emit_bcast_row(p, psA, onec32, rowA, after(ev_rows, ev_AB_free, e_one32))
            eB = emit_bcast_row(p, psB, onec32, rowB, after(ev_rows, ev_AB_free, e_one32))
            ev_s = []
            for m in range(8):
                sl = slice(m * 512, (m + 1) * 512)
                a1 = vop(p, lambda E, sl=sl: E.tensor_tensor(sf[:, sl], y2f[:, sl], psA, ALU.mult),
                         waits=after(eA, ev_y[m]))
                a2 = vop(p, lambda E, sl=sl: E.tensor_tensor(sf[:, sl], sf[:, sl], psB, ALU.subtract),
                         waits=after(eB, a1))
                a3 = vop(p, lambda E, sl=sl, m=m: E.tensor_scalar(sf[:, sl], sf[:, sl],
                                                                  vb[:, 16 + m:16 + m + 1], vb[:, 24 + m:24 + m + 1],
                                                                  ALU.mult, ALU.add),
                         waits=after(a2, lv))
                a4 = vop(p, lambda E, sl=sl, xf=xf: E.tensor_tensor(sf[:, sl], sf[:, sl], xf[:, sl], ALU.add),
                         waits=after(a3, *lxf))
                ev_s.append(a4)
            ev_AB_free = after(*ev_s)
            ev_xf_rd[par] = after(*ev_s)
            sq_ev2, cp_ev2 = [], []
            for m in range(8):
                sq_ev2.append(aop(p, lambda E, m=m: E.activation(sqb[:, m * 512:(m + 1) * 512],
                                                                 sf[:, m * 512:(m + 1) * 512], AF.Square),
                                  waits=after(ev_s[m])))
                cp_ev2.append(vop(p, lambda E, m=m: E.tensor_copy(ybf[:, m * 512:(m + 1) * 512],
                                                                  sf[:, m * 512:(m + 1) * 512]),
                                  waits=after(ev_s[m])))
            ev_rows2 = emit_ln_stats(p, onec,
                                     [ybf[:, m * 512:(m + 1) * 512] for m in range(8)],
                                     [sqb[:, m * 512:(m + 1) * 512] for m in range(8)],
                                     psS, psQ, lambda i: rws[0:1, 1536 + i * 512:1536 + (i + 1) * 512],
                                     rowA2, rowB2, 1e-5,
                                     after(after(*sq_ev2), after(*cp_ev2), ev_stats_free))
            ev_stats_free = ev_rows2
            eA2 = emit_bcast_row(p, psA, onec32, rowA2, after(ev_rows2, ev_AB_free))
            eB2 = emit_bcast_row(p, psB, onec32, rowB2, after(ev_rows2, ev_AB_free))
            ev_fin = []
            sts = []
            for m in range(8):
                sl = slice(m * 512, (m + 1) * 512)
                b1_ = vop(p, lambda E, sl=sl: E.tensor_tensor(y1f[:, sl], sf[:, sl], psA, ALU.mult),
                          waits=after(eA2, ev_s[m], ev_st))
                b2_ = vop(p, lambda E, sl=sl: E.tensor_tensor(y1f[:, sl], y1f[:, sl], psB, ALU.subtract),
                          waits=after(eB2, b1_))
                ev_fin.append(b2_)
                sts.append(st(p, y1[m * 128:(m + 1) * 128, c * 512:(c + 1) * 512],
                              y1f[:, sl], waits=after(b2_)))
            ev_AB_free = after(*ev_fin)
            ev_st = after(*sts)
            ev_chunk = after(ev_st, ev_AB_free)

        ev_ret = after(ev_st, ev_chunk)
    return ev_ret


def emit_s2a(nc, p, stack_outer, dio, B, T, ev0=(), pfx=''):
    TOK = B * T
    NCH = TOK // 512
    CPB = T // 512
    y1T = dio["y1T"]
    exprT = dio["exprT"]
    ctxb = dio["ctxb"]
    ex_tok = dio["ex_tok"]

    wihT = dio["wihT"]
    ctxT = dio["ctxT"]
    xw_tok = dio["xw_tok"]
    h0T = dio["h0T"]

    with ExitStack() as stack:
        e = stack.enter_context
        sb = lambda n, sh, dt=F32: e(nc.sbuf_tensor(pfx + n, sh, dt))[:]
        wih = sb("wih", [128, 8 * 3072], BF16)
        wex = sb("wex", [128, 8 * 1024], BF16)
        wcx = sb("wcx", [128, 8 * 1024], BF16)
        cxb = sb("cxb", [128, 8], F32)
        y1b = sb("y1b", [128, 4096], BF16)
        y1f = sb("y1f", [128, 4096], F32)
        xws = sb("xws", [128, 3072], F32)
        exs = sb("exs", [128, 1024], F32)
        macc = sb("macc", [128, 32], F32)
        idb2 = sb("idb2", [128, 128], F32)
        h0rs = sb("h0rs", [4, 1024], F32)
        mcol = sb("mcol", [128, 1], F32)
        maccb = sb("maccb", [128, 32], BF16)
        h0s = sb("h0s", [128, 32], F32)
        ps = [e(nc.psum_tensor(pfx + f"ps{i}", [128, 512], F32))[:] for i in range(4)]
        psh0 = e(nc.psum_tensor(pfx + "psh0", [128, 1024], F32))[:]

        lw = [ld_bf16(p, wih[:, k * 3072:(k + 1) * 3072], wihT[k * 128:(k + 1) * 128, :]) for k in range(8)]
        lw += [ld_bf16(p, wex[:, k * 1024:(k + 1) * 1024], exprT[k * 128:(k + 1) * 128, :]) for k in range(8)]
        lw += [ld_bf16(p, wcx[:, k * 1024:(k + 1) * 1024], ctxT[k * 128:(k + 1) * 128, :]) for k in range(8)]
        lcb = ld_f32(p, cxb, ctxb)
        ev_w = after(*lw)
        ez = vop(p, lambda E: E.memset(macc, 0.0), waits=ev0)
        ev_macc = ez

        bank_free = [ev0] * 4
        ev_chunk = ev0
        ev_st = None
        for c in range(NCH):
            b = c // CPB
            lx = [ld_bf16(p, y1b[:, k * 512:(k + 1) * 512],
                          y1T[k * 128:(k + 1) * 128, c * 512:(c + 1) * 512], waits=after(ev_chunk))
                  for k in range(8)]
            lxf = [ld_f32(p, y1f[:, k * 512:(k + 1) * 512],
                          y1T[k * 128:(k + 1) * 128, c * 512:(c + 1) * 512], waits=after(ev_chunk))
                   for k in range(8)]
            ev_lx = after(*lx)
            for k in range(8):
                rs = vop(p, lambda E, k=k: E.reduce_sum(mcol, y1f[:, k * 512:(k + 1) * 512], mybir.AxisListType.X),
                         waits=after(*lxf, ev_macc))
                ev_macc = vop(p, lambda E, k=k, b=b: E.tensor_tensor(macc[:, k * 4 + b:k * 4 + b + 1],
                                                                     macc[:, k * 4 + b:k * 4 + b + 1], mcol, ALU.add),
                              waits=after(rs))
            ev_tt_st = []
            for tt in range(4):
                lt = [y1b[:, k * 512 + tt * 128: k * 512 + (tt + 1) * 128] for k in range(8)]
                for jc in range(6):
                    bank = jc % 4
                    g1 = mm_group(p, ps[bank], [(lt[k], wih[:, k * 3072 + jc * 512: k * 3072 + (jc + 1) * 512])
                                                for k in range(8)],
                                  waits=after(ev_lx, ev_w, bank_free[bank]))
                    bank_free[bank] = vop(p, lambda E, bank=bank, jc=jc: E.tensor_copy(
                        xws[:, jc * 512:(jc + 1) * 512], ps[bank]), waits=after(g1, ev_st))
                for jc in range(2):
                    bank = jc % 4
                    g2 = mm_group(p, ps[bank], [(lt[k], wex[:, k * 1024 + jc * 512: k * 1024 + (jc + 1) * 512])
                                                for k in range(8)],
                                  waits=after(ev_lx, ev_w, bank_free[bank]))
                    bank_free[bank] = vop(p, lambda E, bank=bank, jc=jc: E.tensor_copy(
                        exs[:, jc * 512:(jc + 1) * 512], ps[bank]), waits=after(g2, ev_st))
                s1_ = st(p, xw_tok[c * 512 + tt * 128: c * 512 + (tt + 1) * 128, :], xws,
                         waits=after(*[bank_free[i] for i in range(4)]))
                s2_ = st(p, ex_tok[c * 512 + tt * 128: c * 512 + (tt + 1) * 128, :], exs,
                         waits=after(*[bank_free[i] for i in range(4)]), q="scalar")
                ev_tt_st.append(after(s1_, s2_))
                ev_st = after(*ev_tt_st)
            ev_chunk = after(ev_st, ev_macc)
        es = vop(p, lambda E: E.tensor_scalar_mul(macc, macc, 1.0 / T), waits=after(ev_macc))
        eb = vop(p, lambda E: E.tensor_copy(maccb, macc), waits=after(es))
        ev_h0 = []
        for m in range(8):
            g = mm_group(p, ps[m % 4][:, 0:4], [(wcx[:, k * 1024 + m * 128: k * 1024 + (m + 1) * 128],
                                         maccb[:, k * 4:(k + 1) * 4]) for k in range(8)],
                         waits=after(eb, ev_w, bank_free[m % 4]))
            eo = aop(p, lambda E, m=m: E.activation(h0s[:, m * 4:(m + 1) * 4], ps[m % 4][:, 0:4], AF.Identity,
                                                    bias=cxb[:, m:m + 1]), waits=after(g, lcb))
            bank_free[m % 4] = eo
            ev_h0.append(eo)
        sh = st(p, h0T, h0s, waits=after(*ev_h0))
        lid2 = ld_f32(p, idb2, dio["idf"], waits=ev0)
        trs = []
        for k in range(8):
            gt = p.op("tensor", lambda E, k=k: E.transpose(psh0[0:4, k * 128:(k + 1) * 128],
                                                           h0s[:, k * 4:(k + 1) * 4], idb2),
                      waits=after(*ev_h0, lid2, *([("pe", p.cnt.get("pe", 0))] if False else [])), inc="pe")
            trs.append(gt)
        ch = vop(p, lambda E: E.tensor_copy(h0rs[0:4, :], psh0[0:4, :]), waits=after(*trs))
        sh2 = st(p, dio["h0r"], h0rs[0:4, :], waits=after(ch))
        ev_ret = after(sh, sh2, ev_st, *ev_h0)
    return ev_ret


def emit_s2b(nc, p, stack_outer, dio, B, T, nsteps=None, t0=0, ev0=(), pfx=''):
    TOK = B * T
    nsteps = nsteps if nsteps is not None else T
    xw_tok = dio["xw_tok"]
    h0T = dio["h0T"]
    i4 = dio["i4"]
    hTout = dio["hTout"]
    whhT = dio["whhT"]
    h0r = dio["h0r"]
    hs_tok = dio["hs_tok"]
    hrout = dio["hrout"]
    xwflat = xw_tok.rearrange("t j -> (t j)")
    hsflat = hs_tok.rearrange("t j -> (t j)")

    GRP = 4
    assert B == 4

    with ExitStack() as stack:
        e = stack.enter_context
        sb = lambda n, sh, dt=F32: e(nc.sbuf_tensor(pfx + n, sh, dt))[:]
        whh = sb("whh", [128, 8 * 3072], BF16)
        hT = sb("hT", [128, 32], BF16)
        i4b = sb("i4b", [4, 4], F32)
        xwr = sb("xwr", [4, 2 * GRP * 3072], BF16)
        hsb = sb("hsb", [4, 2 * GRP * 1024], F32)
        hw = sb("hw", [4, 3072], F32)
        rsb = sb("rsb", [4, 1024], F32)
        zsb = sb("zsb", [4, 1024], F32)
        nsb = sb("nsb", [4, 1024], F32)
        t1b = sb("t1b", [4, 1024], F32)
        psg = [e(nc.psum_tensor(pfx + f"psg{i}", [128, 512], F32))[:] for i in range(6)]
        pst = e(nc.psum_tensor(pfx + "pst", [128, 32], F32))[:]
        hTf = sb("hTf", [128, 32], F32)

        lw = [ld_bf16(p, whh[:, k * 3072:(k + 1) * 3072], whhT[k * 128:(k + 1) * 128, :]) for k in range(8)]
        lh = ld_bf16(p, hT, h0T)
        ev_w = after(*lw)
        li = ld_f32(p, i4b, i4)

        ev_store = {}
        ev_pref = {}
        ev_gate_done = {}

        def pref(g, waits):
            sl = (g % 2) * GRP * 3072
            evs = []
            for b_ in range(B):
                start = (b_ * T + t0 + g * GRP) * 3072
                evs.append(p.op("gpsimd", lambda E, b_=b_, sl=sl, start=start: E.dma_start(
                    out=xwr[b_:b_ + 1, sl:sl + GRP * 3072],
                    in_=xwflat[start:start + GRP * 3072].rearrange("(a b) -> a b", a=1)), waits=waits, inc="ldg", amt=16))
            return after(*evs)

        ngroups = nsteps // GRP
        ev_pref[0] = pref(0, ())
        ev_pref[1] = pref(1, ())
        ev_lh = after(lh, li)
        ev_hT = ev0

        for step in range(nsteps):
            g = step // GRP
            srel = step % GRP
            sl = (g % 2) * GRP * 3072 + srel * 3072
            hsl = (g % 2) * GRP * 1024 + srel * 1024
            mmw = after(ev_w, ev_hT if step > 0 else ev_lh)
            ev_mm = {}
            first = True
            for jc in (0, 1, 4, 5, 2, 3):
                ev_mm[jc] = mm_group(p, psg[jc][0:4, :],
                                     [(hT[:, k * 4:(k + 1) * 4],
                                       whh[:, k * 3072 + jc * 512: k * 3072 + (jc + 1) * 512]) for k in range(8)],
                                     waits=mmw if first else ())
                first = False
            ev_mm_all = after(*ev_mm.values())
            ev_xw = ev_pref[g]
            ar = {}
            for jc in (0, 1):
                ar[jc] = vop(p, lambda E, jc=jc, sl=sl: E.tensor_tensor(
                    hw[:, jc * 512:(jc + 1) * 512], psg[jc][0:4, :],
                    xwr[0:4, sl + jc * 512: sl + (jc + 1) * 512], ALU.add),
                    waits=after(ev_mm[jc], ev_xw))
            er = aop(p, lambda E: E.activation(rsb, hw[:, 0:1024], AF.Sigmoid), waits=after(ar[0], ar[1]))
            n1 = vop(p, lambda E: E.tensor_tensor(hw[:, 2048:2560], psg[4][0:4, :], rsb[:, 0:512], ALU.mult),
                     waits=after(ev_mm[4], er))
            n2 = vop(p, lambda E: E.tensor_tensor(hw[:, 2560:3072], psg[5][0:4, :], rsb[:, 512:1024], ALU.mult),
                     waits=after(ev_mm[5], er))
            n3 = vop(p, lambda E, sl=sl: E.tensor_tensor(hw[:, 2048:3072], hw[:, 2048:3072],
                                                         xwr[0:4, sl + 2048: sl + 3072], ALU.add),
                     waits=after(n1, n2, ev_xw))
            en = aop(p, lambda E: E.activation(nsb, hw[:, 2048:3072], AF.Tanh), waits=after(n3))
            if step == 0:
                lh0 = ld_f32(p, hsb[0:4, (2 * GRP - 1) * 1024:(2 * GRP) * 1024], h0r, q="scalar")
                prev_sl = (2 * GRP - 1) * 1024
                ev_hprev_ok = after(lh0)
            else:
                prev_sl = ((step - 1) // GRP % 2) * GRP * 1024 + ((step - 1) % GRP) * 1024
                ev_hprev_ok = ev_gate_done.get(step - 1)
            hprev_ap = hsb[0:4, prev_sl:prev_sl + 1024]
            ev_ring_free = ev_store.get(g - 2)
            ar[2] = vop(p, lambda E, sl=sl: E.tensor_tensor(
                hw[:, 1024:1536], psg[2][0:4, :], xwr[0:4, sl + 1024: sl + 1536], ALU.add),
                waits=after(ev_mm[2], ev_xw))
            eza = aop(p, lambda E: E.activation(zsb[:, 0:512], hw[:, 1024:1536], AF.Sigmoid), waits=after(ar[2]))
            w1a = vop(p, lambda E, hprev_ap=hprev_ap: E.tensor_tensor(
                t1b[:, 0:512], hprev_ap[0:4, 0:512], nsb[:, 0:512], ALU.subtract),
                waits=after(en, ev_hprev_ok))
            w1b = vop(p, lambda E, hprev_ap=hprev_ap: E.tensor_tensor(
                t1b[:, 512:1024], hprev_ap[0:4, 512:1024], nsb[:, 512:1024], ALU.subtract),
                waits=after(en, ev_hprev_ok))
            w2a = vop(p, lambda E: E.tensor_tensor(t1b[:, 0:512], zsb[:, 0:512], t1b[:, 0:512], ALU.mult),
                      waits=after(w1a, eza))
            w3a = vop(p, lambda E, hsl=hsl: E.tensor_tensor(hsb[0:4, hsl:hsl + 512], nsb[:, 0:512],
                                                            t1b[:, 0:512], ALU.add),
                      waits=after(w2a, ev_ring_free))
            ar[3] = vop(p, lambda E, sl=sl: E.tensor_tensor(
                hw[:, 1536:2048], psg[3][0:4, :], xwr[0:4, sl + 1536: sl + 2048], ALU.add),
                waits=after(ev_mm[3], ev_xw))
            ezb = aop(p, lambda E: E.activation(zsb[:, 512:1024], hw[:, 1536:2048], AF.Sigmoid), waits=after(ar[3]))
            w2b = vop(p, lambda E: E.tensor_tensor(t1b[:, 512:1024], zsb[:, 512:1024], t1b[:, 512:1024], ALU.mult),
                      waits=after(w1b, ezb))
            w3b = vop(p, lambda E, hsl=hsl: E.tensor_tensor(hsb[0:4, hsl + 512:hsl + 1024], nsb[:, 512:1024],
                                                            t1b[:, 512:1024], ALU.add),
                      waits=after(w2b, ev_ring_free))
            ev_gate_done[step] = after(w3a, w3b)
            ev_tr = []
            for s_ in range(8):
                if s_ == 0:
                    trw = after(w3a, ev_mm_all)
                elif s_ == 4:
                    trw = after(w3b)
                else:
                    trw = ()
                ev_tr.append(p.op("tensor",
                                  lambda E, s_=s_, hsl=hsl: E.matmul(
                                      pst[:, s_ * 4:(s_ + 1) * 4],
                                      hsb[0:4, hsl + s_ * 128: hsl + (s_ + 1) * 128], i4b,
                                      start=True, stop=True),
                                  waits=trw,
                                  inc=("pe" if s_ in (3, 7) else None)))
            ev_hT = vop(p, lambda E: E.tensor_copy(hT, pst), waits=after(ev_tr[3], ev_tr[7]))
            if srel == GRP - 1:
                stsl = (g % 2) * GRP * 1024
                evs = []
                for b_ in range(B):
                    start = (b_ * T + t0 + g * GRP) * 1024
                    evs.append(p.op("scalar", lambda E, b_=b_, stsl=stsl, start=start: E.dma_start(
                        out=hsflat[start:start + GRP * 1024].rearrange("(a b) -> a b", a=1),
                        in_=hsb[b_:b_ + 1, stsl:stsl + GRP * 1024]),
                        waits=after(ev_gate_done[step]), inc="st", amt=16))
                ev_store[g] = after(*evs)
                if g + 2 < ngroups:
                    ev_pref[g + 2] = pref(g + 2, after(ev_gate_done[step]))
        ef = vop(p, lambda E: E.tensor_copy(hTf, pst), waits=after(ev_hT))
        so1 = st(p, hTout, hTf, waits=after(ef))
        last_sl = ((nsteps - 1) // GRP % 2) * GRP * 1024 + ((nsteps - 1) % GRP) * 1024
        so2 = st(p, hrout, hsb[0:4, last_sl:last_sl + 1024], waits=after(ev_gate_done[nsteps - 1]))
        ev_ret = after(ev_store[ngroups - 1], so1, so2)
    return ev_ret


def emit_s2c(nc, p, stack_outer, dio, B, T, total_tokens=65536, ev0=(), pfx=''):
    TOK = B * T
    NT = TOK // 128
    hs_tok = dio["hs_tok"]
    selv = dio["selv"]
    X = mybir.AxisListType.X

    ex_tok = dio["ex_tok"]
    penp = dio["penp"]

    with ExitStack() as stack:
        e = stack.enter_context
        sb = lambda n, sh, dt=F32: e(nc.sbuf_tensor(pfx + n, sh, dt))[:]
        hst = sb("hst", [128, 2048], F32)
        ext = sb("ext", [128, 2048], F32)
        sqf = sb("sqf", [128, 1024], F32)
        rn2 = sb("rn2", [128, 8], F32)
        en2 = sb("en2", [128, 8], F32)
        dot = sb("dot", [128, 8], F32)
        r8a = sb("r8a", [128, 8], F32)
        r8b = sb("r8b", [128, 8], F32)
        r8c = sb("r8c", [128, 8], F32)
        cd = sb("cd", [128, 8], F32)
        rrr = sb("rrr", [128, 8], F32)
        rrA = sb("rrA", [128, 64], F32)
        rrB = sb("rrB", [128, 64], F32)
        Dm = sb("Dm", [128, 64], F32)
        qv = sb("qv", [128, 8], F32)
        ones8 = sb("ones8", [128, 8], F32)
        onesc = sb("onesc", [128, 1], F32)
        scr = sb("scr", [128, 128], F32)
        col1 = sb("col1", [128, 1], F32)
        col2 = sb("col2", [128, 1], F32)
        runc = sb("runc", [128, 1], F32)
        pen_acc = sb("pen_acc", [128, 1], F32)
        selvb = sb("selvb", [128, 24], F32)
        pout = sb("pout", [1, 8], F32)
        psP = e(nc.psum_tensor(pfx + "psP", [128, 512], F32))[:]

        e_ones = vop(p, lambda E: E.memset(ones8, 1.0), waits=ev0)
        e_onec = vop(p, lambda E: E.memset(onesc, 1.0), waits=ev0)
        e_pen = vop(p, lambda E: E.memset(pen_acc, 0.0), waits=ev0)
        ev_ld = [ev0, ev0]
        ev_used = [ev0, ev0]
        ev_stv = ev0
        for i in range(NT):
            par = i % 2
            hsl = slice(par * 1024, par * 1024 + 1024)
            l1 = ld_f32(p, hst[:, hsl], hs_tok[i * 128:(i + 1) * 128, :], waits=after(ev_used[par]))
            l2 = ld_f32(p, ext[:, hsl], ex_tok[i * 128:(i + 1) * 128, :], waits=after(ev_used[par]), q="scalar")
            hti = hst[:, hsl]
            eti = ext[:, hsl]
            a1 = aop(p, lambda E, hti=hti: E.activation(sqf, hti, AF.Square), waits=after(l1))
            a2 = vop(p, lambda E: E.tensor_reduce(rn2, sqf.rearrange("p (e f) -> p e f", e=8), X, ALU.add),
                     waits=after(a1))
            a3 = aop(p, lambda E, eti=eti: E.activation(sqf, eti, AF.Square), waits=after(l2, a2))
            a4 = vop(p, lambda E: E.tensor_reduce(en2, sqf.rearrange("p (e f) -> p e f", e=8), X, ALU.add),
                     waits=after(a3))
            a5 = vop(p, lambda E, hti=hti, eti=eti: E.tensor_tensor(sqf, hti, eti, ALU.mult), waits=after(a4, l1, l2))
            a6 = vop(p, lambda E: E.tensor_reduce(dot, sqf.rearrange("p (e f) -> p e f", e=8), X, ALU.add),
                     waits=after(a5))
            b1 = aop(p, lambda E: E.activation(r8a, rn2, AF.Sqrt), waits=after(a2))
            b2 = aop(p, lambda E: E.activation(r8b, en2, AF.Sqrt), waits=after(a4))
            b3 = vop(p, lambda E: E.tensor_single_scalar(r8c, r8a, 1e-8, ALU.max), waits=after(b1))
            b4 = vop(p, lambda E: E.tensor_single_scalar(r8b, r8b, 1e-8, ALU.max), waits=after(b2))
            b5 = vop(p, lambda E: E.tensor_tensor(r8c, r8c, r8b, ALU.mult), waits=after(b3, b4))
            b6 = vop(p, lambda E: E.reciprocal(r8c, r8c), waits=after(b5))
            b7 = vop(p, lambda E: E.tensor_tensor(cd, dot, r8c, ALU.mult), waits=after(a6, b6))
            b8 = vop(p, lambda E: E.tensor_scalar(cd, cd, -1.0, 1.0, ALU.mult, ALU.add), waits=after(b7))
            c1 = vop(p, lambda E: E.tensor_single_scalar(rrr, r8a, 1e-12, ALU.max), waits=after(b1))
            c2 = vop(p, lambda E: E.reciprocal(rrr, rrr), waits=after(c1))
            evA = []
            for e_ in range(8):
                evA.append(vop(p, lambda E, e_=e_: E.tensor_scalar_mul(rrA[:, e_ * 8:(e_ + 1) * 8], ones8,
                                                                       rrr[:, e_:e_ + 1]),
                               waits=after(c2, e_ones)))
                evA.append(aop(p, lambda E, e_=e_: E.activation(rrB[:, e_ * 8:(e_ + 1) * 8], rrr,
                                                                AF.Identity), waits=after(c2)))
            evD = []
            for e_ in range(8):
                evD.append(aop(p, lambda E, e_=e_: E.activation(Dm[:, e_ * 9:e_ * 9 + 1], rn2[:, e_:e_ + 1],
                                                                AF.Identity), waits=after(a2)))
                for f_ in range(e_ + 1, 8):
                    ed = vop(p, lambda E, e_=e_, f_=f_, hti=hti: E.scalar_tensor_tensor(
                        scr, hti[:, e_ * 128:(e_ + 1) * 128], 1.0, hti[:, f_ * 128:(f_ + 1) * 128],
                        ALU.mult, ALU.mult, accum_out=Dm[:, e_ * 8 + f_:e_ * 8 + f_ + 1]), waits=after(l1))
                    evD.append(aop(p, lambda E, e_=e_, f_=f_: E.activation(
                        Dm[:, f_ * 8 + e_:f_ * 8 + e_ + 1], Dm[:, e_ * 8 + f_:e_ * 8 + f_ + 1],
                        AF.Identity), waits=after(ed)))
            d1 = vop(p, lambda E: E.tensor_tensor(Dm, Dm, rrA, ALU.mult), waits=after(*evA, *evD))
            d2 = vop(p, lambda E: E.tensor_tensor(Dm, Dm, rrB, ALU.mult), waits=after(d1))
            evdg = []
            for e_ in range(8):
                evdg.append(vop(p, lambda E, e_=e_: E.tensor_scalar_add(Dm[:, e_ * 9:e_ * 9 + 1],
                                                                        Dm[:, e_ * 9:e_ * 9 + 1], -1.0),
                                waits=after(d2)))
            d3 = vop(p, lambda E: E.tensor_tensor(Dm, Dm, Dm, ALU.mult), waits=after(*evdg))
            d4 = vop(p, lambda E: E.tensor_reduce(qv, Dm.rearrange("p (e f) -> p e f", e=8), X, ALU.add),
                     waits=after(d3))
            d5 = aop(p, lambda E: E.activation(r8a, qv, AF.Sqrt), waits=after(d4))
            d6 = vop(p, lambda E: E.tensor_single_scalar(r8a, r8a, 1e-12, ALU.max), waits=after(d5))
            d7 = vop(p, lambda E: E.tensor_tensor(r8a, r8a, r8a, ALU.mult), waits=after(d6))
            d8 = vop(p, lambda E: E.reciprocal(r8a, r8a), waits=after(d7))
            d9 = vop(p, lambda E: E.tensor_tensor(r8a, qv, r8a, ALU.mult), waits=after(d8))
            d10 = vop(p, lambda E: E.tensor_reduce(col1, r8a, X, ALU.add), waits=after(d9))
            ev_pen2 = vop(p, lambda E: E.tensor_tensor(pen_acc, pen_acc, col1, ALU.add), waits=after(d10, e_pen))
            e_pen = ev_pen2
            sel1 = selvb[:, 0:8]
            sel2 = selvb[:, 8:16]
            m1c = selvb[:, 16:17]
            m2c = selvb[:, 17:18]
            t1_ = vop(p, lambda E: E.reduce_max(m1c, cd, X), waits=after(b8, ev_stv))
            rz = vop(p, lambda E: E.memset(runc, 0.0), waits=after(ev_stv))
            for e_ in range(8):
                q1 = vop(p, lambda E, e_=e_: E.tensor_tensor(col1, cd[:, e_:e_ + 1], m1c, ALU.is_equal),
                         waits=after(t1_, rz))
                q2 = vop(p, lambda E: E.tensor_scalar(col2, runc, -1.0, 1.0, ALU.mult, ALU.add), waits=after(q1))
                q3 = vop(p, lambda E, e_=e_: E.tensor_tensor(sel1[:, e_:e_ + 1], col1, col2, ALU.mult),
                         waits=after(q2))
                rz = vop(p, lambda E, e_=e_: E.tensor_tensor(runc, runc, sel1[:, e_:e_ + 1], ALU.add),
                         waits=after(q3))
            cm = vop(p, lambda E: E.scalar_tensor_tensor(Dm[:, 0:8], sel1, -1e30, cd, ALU.mult, ALU.add),
                     waits=after(rz))
            t2_ = vop(p, lambda E: E.reduce_max(m2c, Dm[:, 0:8], X), waits=after(cm))
            rz2 = vop(p, lambda E: E.memset(runc, 0.0), waits=after(t2_))
            for e_ in range(8):
                q1 = vop(p, lambda E, e_=e_: E.tensor_tensor(col1, Dm[:, e_:e_ + 1], m2c, ALU.is_equal),
                         waits=after(t2_, rz2))
                q2 = vop(p, lambda E: E.tensor_scalar(col2, runc, -1.0, 1.0, ALU.mult, ALU.add), waits=after(q1))
                q3 = vop(p, lambda E, e_=e_: E.tensor_tensor(sel2[:, e_:e_ + 1], col1, col2, ALU.mult),
                         waits=after(q2))
                rz2 = vop(p, lambda E, e_=e_: E.tensor_tensor(runc, runc, sel2[:, e_:e_ + 1], ALU.add),
                          waits=after(q3))
            ev_z = vop(p, lambda E: E.memset(selvb[:, 18:24], 0.0), waits=after(ev_stv))
            ev_stv = st(p, selv[i * 128:(i + 1) * 128, :], selvb, waits=after(rz2, ev_z, ev_pen2))
            ev_used[par] = after(a5, *evD)
        ez_p = vop(p, lambda E: E.memset(pout, 0.0), waits=ev0)
        gm = mm_group(p, psP[0:1, 0:1], [(onesc, pen_acc)], waits=after(e_pen, e_onec))
        gf = vop(p, lambda E: E.tensor_scalar_mul(pout[0:1, 0:1], psP[0:1, 0:1], 1.0 / total_tokens),
                 waits=after(gm, ez_p))
        sp = st(p, penp, pout, waits=after(gf))
        ev_ret = after(sp, ev_stv)
    return ev_ret


def emit_ccpen(nc, p, stack_outer, dio, B, T, ev0=(), pfx=''):
    ev = p.op("gpsimd", lambda E: E.collective_compute(
        "AllReduce", ALU.add, replica_groups=[list(range(NCORES))],
        ins=[dio["penp"].opt()], outs=[dio["pensum"].opt()]),
        waits=ev0, inc="ccs", amt=1, amb=True)
    return after(ev)


def emit_s2d0(nc, p, stack_outer, dio, B, T, ev0=(), pfx=''):
    TOK = B * T
    NCH = TOK // 512
    selv = dio["selv"]
    idf = dio["idf"]

    pensum = dio["pensum"]
    cwT = dio["cwT"]

    with ExitStack() as stack:
        e = stack.enter_context
        sb = lambda n, sh, dt=F32: e(nc.sbuf_tensor(pfx + n, sh, dt))[:]
        svb = sb("svb", [128, 24], F32)
        pen1 = sb("pen1", [1, 8], F32)
        fct = sb("fct", [128, 1], F32)
        cw2 = sb("cw2", [128, 8], F32)
        onesr = sb("onesr", [1, 128], F32)
        idb = sb("idb", [128, 128], F32)
        cw = sb("cw", [128, 8], F32)
        col1 = sb("col1", [128, 1], F32)
        w1c = sb("w1c", [128, 1], F32)
        w2c = sb("w2c", [128, 1], F32)
        cwr = sb("cwr", [8, 512], F32)
        rowt = sb("rowt", [1, 4096], F32)
        pse = [e(nc.psum_tensor(pfx + f"pse{i}", [128, 512], F32))[:] for i in range(8)]

        lp = ld_f32(p, pen1, pensum)
        lid = ld_f32(p, idb, idf)
        e_or = vop(p, lambda E: E.memset(onesr, 1.0), waits=ev0)
        gb = mm_group(p, pse[0][:, 0:1], [(onesr, pen1[0:1, 0:1])], waits=after(lp, e_or))
        gf = vop(p, lambda E: E.tensor_scalar_add(fct, pse[0][:, 0:1], 1.0), waits=after(gb))

        ev_tr = {}
        ev_ev = after(ev0, gf)
        ev_sv_free = None
        ev_cw_free = ev0
        for c in range(NCH):
            for tt in range(4):
                i = c * 4 + tt
                lv = ld_f32(p, svb, selv[i * 128:(i + 1) * 128, :], waits=after(ev_sv_free))
                x1 = vop(p, lambda E: E.tensor_tensor(col1, svb[:, 16:17], svb[:, 17:18], ALU.subtract),
                         waits=after(lv))
                x2 = vop(p, lambda E: E.tensor_scalar_mul(col1, col1, fct[:, 0:1]), waits=after(x1, gf))
                x3 = aop(p, lambda E: E.activation(w1c, col1, AF.Sigmoid), waits=after(x2))
                x4 = vop(p, lambda E: E.tensor_scalar(w2c, w1c, -1.0, 1.0, ALU.mult, ALU.add), waits=after(x3))
                x5 = vop(p, lambda E: E.tensor_scalar_mul(cw2, svb[:, 0:8], w1c[:, 0:1]), waits=after(x4))
                x6 = vop(p, lambda E: E.scalar_tensor_tensor(cw, svb[:, 8:16], w2c[:, 0:1], cw2, ALU.mult, ALU.add),
                         waits=after(x5, ev_cw_free))
                ev_sv_free = x6
                trs = []
                for e_ in range(8):
                    ev_tr[(e_, tt)] = p.op("tensor",
                                           lambda E, e_=e_, tt=tt: E.transpose(
                                               pse[e_][0:1, tt * 128:(tt + 1) * 128], cw[:, e_:e_ + 1], idb),
                                           waits=after(x6, lid, ev_ev), inc="pe")
                    trs.append(ev_tr[(e_, tt)])
                ev_cw_free = after(*trs)
            evs = []
            for e_ in range(8):
                ee = vop(p, lambda E, e_=e_: E.tensor_copy(rowt[0:1, e_ * 512:(e_ + 1) * 512], pse[e_][0:1, :]),
                         waits=after(*[ev_tr[(e_, tt)] for tt in range(4)]))
                evs.append(st(p, cwT[e_:e_ + 1, c * 512:(c + 1) * 512], rowt[0:1, e_ * 512:(e_ + 1) * 512],
                              waits=after(ee)))
                evs.append(ee)
            ev_ev = after(*evs)
        ev_ret = after(ev_ev)
    return ev_ret


def emit_s2d(nc, p, stack_outer, dio, B, T, nexp=9, ev0=(), pfx=''):
    TOK = B * T
    NCH = TOK // 512
    y1T = dio["y1T"]
    gT = dio["gT"]
    dT = dio["dT"]
    suT = dio["suT"]
    accT = dio["accT"]

    cwT = dio["cwT"]
    sgT = dio["sgT"]
    sdT = dio["sdT"]

    uT = dio["uT"]

    with ExitStack() as stack:
        e = stack.enter_context
        sb = lambda n, sh, dt=F32: e(nc.sbuf_tensor(pfx + n, sh, dt))[:]
        gw = sb("gw", [128, 8 * 2048], BF16)
        uw = sb("uw", [128, 8 * 2048], BF16)
        dw = sb("dw", [128, 16 * 1024], BF16)
        gw2 = sb("gw2", [128, 8 * 2048], BF16)
        y1b_ = [sb("y1b", [128, 4096], BF16), sb("y1b2", [128, 4096], BF16)]
        y1c = sb("y1c", [128, 4096], BF16)
        cwrow = sb("cwrow", [1, 1024], F32)
        cwsb = sb("cwsb", [128, 512], F32)
        onesr = sb("onesr", [1, 128], F32)
        gsb = sb("gsb", [128, 1024], BF16)
        actb = sb("actb", [128, 16 * 512], BF16)
        dsb = sb("dsb", [128, 1024], F32)
        psG = [e(nc.psum_tensor(pfx + f"psG{i}", [128, 512], F32))[:] for i in range(2)]
        psU = [e(nc.psum_tensor(pfx + f"psU{i}", [128, 512], F32))[:] for i in range(2)]
        psD = [e(nc.psum_tensor(pfx + f"psD{i}", [128, 512], F32))[:] for i in range(2)]
        psC = e(nc.psum_tensor(pfx + "psC", [128, 512], F32))[:]

        e_or = vop(p, lambda E: E.memset(onesr, 1.0))
        ev_wfree = ev0
        ev_acc = {}
        gfree = [ev0, ev0]
        ufree = [ev0, ev0]
        dfree = [ev0, ev0]
        cfree = ev0
        dsb_free = [ev0, ev0]
        gsb_free = [ev0, ev0]
        ev_y1_free = [ev0, ev0]
        ev_up_read = ev0
        ev_act_read = ev0
        ev_cwslot = [ev0, ev0]
        gw_ = [gw, gw2]
        ev_gw_read = [ev0, ev0]
        ev_uw_read = ev0
        ev_dw_read = ev0
        NCHUNKS_ALL = nexp * NCH

        def emit_ly(ci2):
            c2 = ci2 % NCH
            buf = y1b_[ci2 % 2]
            return [ld_bf16(p, buf[:, k * 512:(k + 1) * 512],
                            y1T[k * 128:(k + 1) * 128, c2 * 512:(c2 + 1) * 512],
                            waits=after(ev_y1_free[ci2 % 2])) for k in range(8)]

        ev_ly_next = after(*emit_ly(0))
        ci = 0
        for e_ in range(nexp):
            shared = (e_ == nexp - 1)
            gwb = gw_[e_ % 2]
            lg = [ld_bf16(p, gwb[:, k * 2048:(k + 1) * 2048],
                          (sgT if shared else gT[e_])[k * 128:(k + 1) * 128, :],
                          waits=after(ev_gw_read[e_ % 2]), q="scalar") for k in range(8)]
            lu = [ld_bf16(p, uw[:, k * 2048:(k + 1) * 2048],
                          (suT if shared else uT[e_])[k * 128:(k + 1) * 128, :],
                          waits=after(ev_uw_read), q="scalar") for k in range(8)]
            ldw = [ld_bf16(p, dw[:, k * 1024:(k + 1) * 1024],
                           (sdT if shared else dT[e_])[k * 128:(k + 1) * 128, :],
                           waits=after(ev_dw_read), q="sync") for k in range(16)]
            ev_gw = after(*lg)
            ev_uw = after(*lu)
            ev_dw = after(*ldw)
            g1s_exp, g2s_exp, g3s_exp = [], [], []
            last_mm = None
            for c in range(NCH):
                y1b = y1b_[ci % 2]
                ev_ly = ev_ly_next
                if ci + 1 < NCHUNKS_ALL:
                    ev_ly_next = after(*emit_ly(ci + 1))
                ev_y1_rd = []
                ev_g2s = []
                if not shared:
                    sl_cw = slice((c % 2) * 512, (c % 2) * 512 + 512)
                    lcw = ld_f32(p, cwrow[0:1, sl_cw], cwT[e_:e_ + 1, c * 512:(c + 1) * 512],
                                 waits=after(ev_cwslot[c % 2]))
                    gb = mm_group(p, psC, [(onesr, cwrow[0:1, sl_cw])],
                                  waits=after(lcw, e_or, cfree))
                    ec = vop(p, lambda E: E.tensor_copy(cwsb, psC), waits=after(gb))
                    cfree = ec
                    ev_cwslot[c % 2] = gb
                    ev_y1c = []
                    for k in range(8):
                        ev_y1c.append(vop(p, lambda E, k=k, y1b=y1b: E.tensor_tensor(
                            y1c[:, k * 512:(k + 1) * 512], y1b[:, k * 512:(k + 1) * 512], cwsb, ALU.mult),
                            waits=after(ec, ev_ly, ev_up_read)))
                    ev_up_in = after(*ev_y1c)
                    up_in = y1c
                    ev_y1_rd += ev_y1c
                else:
                    ev_up_in = ev_ly
                    up_in = y1b
                ev_act = []
                for fm in range(16):
                    bk = fm % 2
                    g1 = mm_group(p, psG[bk], [(gwb[:, k * 2048 + fm * 128: k * 2048 + fm * 128 + 128],
                                                y1b[:, k * 512:(k + 1) * 512]) for k in range(8)],
                                  waits=after(ev_ly, ev_gw, gfree[bk]))
                    g2 = mm_group(p, psU[bk], [(uw[:, k * 2048 + fm * 128: k * 2048 + fm * 128 + 128],
                                                up_in[:, k * 512:(k + 1) * 512]) for k in range(8)],
                                  waits=after(ev_up_in, ev_uw, ufree[bk]))
                    ev_y1_rd.append(g1)
                    ev_g2s.append(g2)
                    g1s_exp.append(g1)
                    g2s_exp.append(g2)
                    if shared:
                        ev_y1_rd.append(g2)
                    es = aop(p, lambda E, bk=bk: E.activation(gsb[:, bk * 512:(bk + 1) * 512], psG[bk], AF.Silu),
                             waits=after(g1, gsb_free[bk]))
                    ea = vop(p, lambda E, bk=bk, fm=fm: E.tensor_tensor(
                        actb[:, fm * 512:(fm + 1) * 512], gsb[:, bk * 512:(bk + 1) * 512], psU[bk], ALU.mult),
                        waits=after(es, g2, ev_act_read))
                    gfree[bk] = es
                    ufree[bk] = ea
                    gsb_free[bk] = ea
                    ev_act.append(ea)
                ev_acta = after(*ev_act)
                ev_g3s = []
                for dm in range(8):
                    bk = dm % 2
                    g3 = mm_group(p, psD[bk], [(dw[:, k * 1024 + dm * 128: k * 1024 + dm * 128 + 128],
                                                actb[:, k * 512:(k + 1) * 512]) for k in range(16)],
                                  waits=after(ev_acta, ev_dw, dfree[bk]))
                    ev_g3s.append(g3)
                    g3s_exp.append(g3)
                    ed = vop(p, lambda E, bk=bk: E.tensor_copy(dsb[:, bk * 512: bk * 512 + 512],
                                                               psD[bk]), waits=after(g3, dsb_free[bk]))
                    dfree[bk] = ed
                    ea2 = p.op("gpsimd", lambda E, dm=dm, c=c, e_=e_, bk=bk: E.dma_start(
                        out=accT[dm * 128:(dm + 1) * 128, c * 512:(c + 1) * 512],
                        in_=dsb[:, bk * 512: bk * 512 + 512],
                        accum_op=(ALU.bypass if e_ == 0 else ALU.add)),
                        waits=after(ed, ev_acc.get((c, dm))), inc="st", amt=16)
                    ev_acc[(c, dm)] = ea2
                    dsb_free[bk] = ea2
                    last_mm = g3
                ev_y1_free[ci % 2] = after(*ev_y1_rd)
                ev_up_read = after(*ev_g2s)
                ev_act_read = after(*ev_g3s)
                ci += 1
            ev_gw_read[e_ % 2] = after(*g1s_exp)
            ev_uw_read = after(*g2s_exp)
            ev_dw_read = after(*g3s_exp)
        ev_ret = after(*[ev_acc[(c, dm)] for c in range(NCH) for dm in range(8)])
    return ev_ret


def emit_s2e(nc, p, stack_outer, dio, B, T, ev0=(), pfx=''):
    TOK = B * T
    NCH = TOK // 512
    y1T = dio["y1T"]
    y2T = dio["y2T"]

    accT = dio["accT"]

    with ExitStack() as stack:
        e = stack.enter_context
        sb = lambda n, sh, dt=F32: e(nc.sbuf_tensor(pfx + n, sh, dt))[:]
        y1f = sb("y1f", [128, 4096], F32)
        acf = sb("acf", [128, 4096], F32)
        sf = sb("sf", [128, 4096], F32)
        sqb = sb("sqb", [128, 4096], BF16)
        ybf = sb("ybf", [128, 4096], BF16)
        outf = sb("outf", [128, 4096], F32)
        rowA = sb("rowA", [1, 512], F32)
        rowB = sb("rowB", [1, 512], F32)
        rws = sb("rws", [1, 2048], F32)
        onec = sb("onec", [128, 1], BF16)
        onec32 = sb("onec32", [1, 128], F32)
        psS = e(nc.psum_tensor(pfx + "psS", [128, 512], F32))[:]
        psQ = e(nc.psum_tensor(pfx + "psQ", [128, 512], F32))[:]
        psA = e(nc.psum_tensor(pfx + "psA", [128, 512], F32))[:]
        psB = e(nc.psum_tensor(pfx + "psB", [128, 512], F32))[:]
        e_one = gop(p, lambda E: E.memset(onec, 1.0))
        e_one32 = gop(p, lambda E: E.memset(onec32, 1.0))

        ev_chunk = ev0
        ev_st = None
        for c in range(NCH):
            l1 = [ld_f32(p, y1f[:, k * 512:(k + 1) * 512],
                         y1T[k * 128:(k + 1) * 128, c * 512:(c + 1) * 512], waits=after(ev_chunk)) for k in range(8)]
            l2 = [ld_f32(p, acf[:, k * 512:(k + 1) * 512],
                         accT[k * 128:(k + 1) * 128, c * 512:(c + 1) * 512], waits=after(ev_chunk), q="scalar")
                  for k in range(8)]
            ev_s, sq_ev, cp_ev = [], [], []
            for m in range(8):
                sl = slice(m * 512, (m + 1) * 512)
                s1 = vop(p, lambda E, sl=sl: E.tensor_tensor(sf[:, sl], y1f[:, sl], acf[:, sl], ALU.add),
                         waits=after(*l1, *l2))
                ev_s.append(s1)
                sq_ev.append(aop(p, lambda E, sl=sl: E.activation(sqb[:, sl], sf[:, sl], AF.Square),
                                 waits=after(s1)))
                cp_ev.append(vop(p, lambda E, sl=sl: E.tensor_copy(ybf[:, sl], sf[:, sl]), waits=after(s1)))
            ev_rows = emit_ln_stats(p, onec,
                                    [ybf[:, m * 512:(m + 1) * 512] for m in range(8)],
                                    [sqb[:, m * 512:(m + 1) * 512] for m in range(8)],
                                    psS, psQ, lambda i: rws[0:1, i * 512:(i + 1) * 512],
                                    rowA, rowB, 1e-5, after(after(*sq_ev), after(*cp_ev), e_one, ev_chunk))
            eA = emit_bcast_row(p, psA, onec32, rowA, after(ev_rows, e_one32, ev_chunk))
            eB = emit_bcast_row(p, psB, onec32, rowB, after(ev_rows, e_one32, ev_chunk))
            ev_f = []
            sts = []
            for m in range(8):
                sl = slice(m * 512, (m + 1) * 512)
                f1 = vop(p, lambda E, sl=sl: E.tensor_tensor(outf[:, sl], sf[:, sl], psA, ALU.mult),
                         waits=after(eA, ev_s[m], ev_st))
                f2 = vop(p, lambda E, sl=sl: E.tensor_tensor(outf[:, sl], outf[:, sl], psB, ALU.subtract),
                         waits=after(eB, f1))
                ev_f.append(f2)
                sts.append(st(p, y2T[m * 128:(m + 1) * 128, c * 512:(c + 1) * 512], outf[:, sl], waits=after(f2)))
            ev_st = after(*sts)
            ev_chunk = after(ev_st, *ev_f)
        ev_ret = after(ev_st, ev_chunk)
    return ev_ret


def emit_s3a(nc, p, stack_outer, dio, B, T, ev0=(), pfx=''):
    TOK = B * T
    NCH = TOK // 512
    CPB = T // 512
    y2T = dio["y2T"]
    qkvb = dio["qkvb"]
    ropeS = dio["ropeS"]
    exp2 = dio["exp2"]
    qTo = dio["qTo"]
    vtok = dio["vtok"]

    qkvT = dio["qkvT"]
    ropeC = dio["ropeC"]
    sel2e = dio["sel2e"]
    idf = dio["idf"]
    kTo = dio["kTo"]

    with ExitStack() as stack:
        e = stack.enter_context
        sb = lambda n, sh, dt=F32: e(nc.sbuf_tensor(pfx + n, sh, dt))[:]
        wq = sb("wq", [128, 8 * 3072], BF16)
        bq = sb("bq", [128, 24], F32)
        cC = sb("cC", [128, T], BF16)
        cS = sb("cS", [128, T], BF16)
        se = sb("se", [128, 4], BF16)
        ex2 = sb("ex2", [2, 128], BF16)
        ident_b = sb("ident_b", [128, 128], BF16)
        y2b_ = [sb("y2b", [128, 4096], BF16), sb("y2c", [128, 4096], BF16)]
        qf = sb("qf", [128, 512], F32)
        t1 = sb("t1", [128, 512], F32)
        t2 = sb("t2", [128, 512], F32)
        qp = sb("qp", [128, 512], F32)
        sq = sb("sq", [128, 512], BF16)
        qstage = sb("qstage", [128, 16 * 512], BF16)
        nrm = sb("nrm", [2, 8192], F32)
        nrmb = sb("nrmb", [2, 8192], BF16)
        qhat = sb("qhat", [128, 512], BF16)
        vsq = sb("vsq", [128, 512], BF16)
        vstage = sb("vstage", [128, 4096], BF16)
        ps = [e(nc.psum_tensor(pfx + f"ps{i}", [128, 512], F32))[:] for i in range(4)]
        psN = e(nc.psum_tensor(pfx + "psN", [128, 512], F32))[:]
        psB_ = e(nc.psum_tensor(pfx + "psB_", [128, 512], F32))[:]
        psT = e(nc.psum_tensor(pfx + "psT", [128, 128], BF16))[:]

        lw = [ld_bf16(p, wq[:, k * 3072:(k + 1) * 3072], qkvT[k * 128:(k + 1) * 128, :]) for k in range(8)]
        lb = ld_f32(p, bq, qkvb)
        lc = [ld_bf16(p, cC, ropeC), ld_bf16(p, cS, ropeS), ld_bf16(p, se, sel2e), ld_bf16(p, ex2, exp2),
              ld_bf16(p, ident_b, idf)]
        ev_w = after(*lw)
        ev_c = after(*lc)

        bank_free = [ev0] * 4
        psN_free = ev0
        psB_free = None
        psT_free = ev0
        ev_stage_free = None
        ev_nrm_used = ev0
        ev_vst_free = None
        ev_sq_free = ev0
        ev_qhat_free = None
        ev_qf_free = ev0
        ev_y2_free = [ev0, ev0]
        for c in range(NCH):
            tc = c % CPB
            y2b = y2b_[c % 2]
            ly = [ld_bf16(p, y2b[:, k * 512:(k + 1) * 512],
                          y2T[k * 128:(k + 1) * 128, c * 512:(c + 1) * 512],
                          waits=after(ev_y2_free[c % 2])) for k in range(8)]
            ev_ly = after(*ly)
            ev_y2_rd = []
            ev_stage = []
            for mi in range(16):
                bank = mi % 4
                g1 = mm_group(p, ps[bank], [(wq[:, k * 3072 + mi * 128: k * 3072 + (mi + 1) * 128],
                                             y2b[:, k * 512:(k + 1) * 512]) for k in range(8)],
                              waits=after(ev_ly, ev_w, bank_free[bank]))
                ev_y2_rd.append(g1)
                Ct = cC[:, tc * 512:(tc + 1) * 512]
                St = cS[:, tc * 512:(tc + 1) * 512]
                e0 = aop(p, lambda E, bank=bank, mi=mi: E.activation(qf, ps[bank], AF.Identity,
                                                                     bias=bq[:, mi:mi + 1]),
                         waits=after(g1, lb, ev_stage_free, ev_qf_free))
                bank_free[bank] = e0
                r1 = vop(p, lambda E, Ct=Ct: E.tensor_tensor(t1, qf, Ct, ALU.mult), waits=after(e0, ev_c))
                rs = []
                for dh, sh in ((0, 32), (32, 0), (64, 96), (96, 64)):
                    rs.append(vop(p, lambda E, dh=dh, sh=sh, St=St: E.tensor_tensor(
                        t2[dh:dh + 32, :], qf[sh:sh + 32, :], St[sh:sh + 32, :], ALU.mult),
                        waits=after(e0, ev_c)))
                r3 = vop(p, lambda E: E.tensor_tensor(qp[0:32, :], t1[0:32, :], t2[0:32, :], ALU.subtract),
                         waits=after(r1, *rs))
                r4 = vop(p, lambda E: E.tensor_tensor(qp[32:64, :], t1[32:64, :], t2[32:64, :], ALU.add),
                         waits=after(r1, *rs))
                r5 = vop(p, lambda E: E.tensor_tensor(qp[64:96, :], t1[64:96, :], t2[64:96, :], ALU.subtract),
                         waits=after(r1, *rs))
                r6 = vop(p, lambda E: E.tensor_tensor(qp[96:128, :], t1[96:128, :], t2[96:128, :], ALU.add),
                         waits=after(r1, *rs))
                ev_rope = after(r3, r4, r5, r6)
                ev_qf_free = after(r1, *rs)
                st_ = vop(p, lambda E, mi=mi: E.tensor_copy(qstage[:, mi * 512:(mi + 1) * 512], qp),
                          waits=after(ev_rope))
                ev_stage.append(st_)
                s1 = aop(p, lambda E: E.activation(sq, qp, AF.Square), waits=after(ev_rope))
                gn = mm_group(p, psN[0:2, :], [(se[:, 0:2], sq), (se[:, 2:4], sq)],
                              waits=after(s1, ev_c, psN_free))
                psN_free = vop(p, lambda E, mi=mi: E.tensor_copy(nrm[0:2, mi * 512:(mi + 1) * 512], psN[0:2, :]),
                               waits=after(gn))
            n1 = aop(p, lambda E: E.activation(nrm, nrm, AF.Sqrt), waits=after(psN_free, ev_nrm_used))
            n2 = vop(p, lambda E: E.tensor_single_scalar(nrm, nrm, 1e-12, ALU.max), waits=after(n1))
            n3 = vop(p, lambda E: E.reciprocal(nrm, nrm), waits=after(n2))
            n4 = vop(p, lambda E: E.tensor_scalar_mul(nrm, nrm, float(0.125 ** 0.5)), waits=after(n3))
            n5 = vop(p, lambda E: E.tensor_copy(nrmb, nrm), waits=after(n4))
            ev_app = []
            for mi in range(16):
                gB = mm_group(p, psB_, [(ex2, nrmb[0:2, mi * 512:(mi + 1) * 512])],
                              waits=after(n5, ev_c, psB_free))
                ap_ = vop(p, lambda E, mi=mi: E.tensor_tensor(qhat, qstage[:, mi * 512:(mi + 1) * 512],
                                                              psB_, ALU.mult),
                          waits=after(gB, ev_stage[mi], ev_qhat_free))
                psB_free = ap_
                dst = qTo if mi < 8 else kTo
                mo = mi % 8
                ev_qhat_free = st(p, dst[mo * 128:(mo + 1) * 128, c * 512:(c + 1) * 512], qhat, waits=after(ap_))
                ev_app.append(ev_qhat_free)
                ev_app.append(ap_)
            ev_stage_free = after(*ev_app)
            ev_nrm_used = after(*[x for x in ev_app])
            ev_vst = []
            for m in range(8):
                mi = 16 + m
                bank = m % 4
                g1 = mm_group(p, ps[bank], [(wq[:, k * 3072 + mi * 128: k * 3072 + (mi + 1) * 128],
                                             y2b[:, k * 512:(k + 1) * 512]) for k in range(8)],
                              waits=after(ev_ly, ev_w, bank_free[bank]))
                ev_y2_rd.append(g1)
                ev_b = aop(p, lambda E, bank=bank, mi=mi: E.activation(vsq, ps[bank], AF.Identity,
                                                                      bias=bq[:, mi:mi + 1]),
                           waits=after(g1, lb, ev_vst_free, ev_sq_free))
                bank_free[bank] = ev_b
                for tt in range(4):
                    gt = p.op("tensor", lambda E, tt=tt: E.transpose(psT, vsq[:, tt * 128:(tt + 1) * 128], ident_b),
                              waits=after(ev_b, psT_free, ev_c), inc="pe")
                    ev_sq_free = gt
                    psT_free = vop(p, lambda E, m=m, tt=tt: E.tensor_copy(
                        vstage[:, tt * 1024 + m * 128: tt * 1024 + (m + 1) * 128], psT), waits=after(gt))
                    ev_vst.append(psT_free)
            sts = []
            for tt in range(4):
                sts.append(st(p, vtok[c * 512 + tt * 128: c * 512 + (tt + 1) * 128, :],
                              vstage[:, tt * 1024:(tt + 1) * 1024], waits=after(*ev_vst)))
            ev_vst_free = after(*sts, *ev_vst)
            ev_y2_free[c % 2] = after(*ev_y2_rd)

        ev_ret = after(ev_stage_free, ev_vst_free)
    return ev_ret


def emit_s3b(nc, p, stack_outer, dio, B, T, ev0=(), pfx=''):
    TOK = B * T
    NKT = T // 128
    NQC = T // 512
    qT = dio["qT"]
    vtok = dio["vtok"]

    kT = dio["kT"]
    aoT = dio["aoT"]

    with ExitStack() as stack:
        e = stack.enter_context
        sb = lambda n, sh, dt=F32: e(nc.sbuf_tensor(pfx + n, sh, dt))[:]
        qh = sb("qh", [64, T], BF16)
        kh = sb("kh", [64, T], BF16)
        vtb = sb("vtb", [128, NKT * 65], BF16)
        at = sb("at", [128, NKT * T], BF16)
        o64_ = [sb("o64", [64, 512], BF16), sb("o64b", [64, 512], BF16)]
        aob = sb("aob", [64, 512], BF16)
        rden_ = [sb("rden", [1, 512], F32), sb("rdenb", [1, 512], F32)]
        rdb_ = [sb("rdb", [1, 512], BF16), sb("rdbb", [1, 512], BF16)]
        ones64 = sb("ones64", [1, 64], BF16)
        psS = [e(nc.psum_tensor(pfx + f"psS{i}", [128, 512], F32))[:] for i in range(2)]
        psV = [e(nc.psum_tensor(pfx + f"psV{i}", [128, 512], F32))[:] for i in range(2)]
        psB_ = e(nc.psum_tensor(pfx + "psB_", [128, 512], F32))[:]

        ev_ones = gop(p, lambda E: E.memset(vtb, 1.0))
        e_o64 = gop(p, lambda E: E.memset(ones64, 1.0))
        ev_qk_free = after(ev_ones)
        ev_av_free = after(ev_ones)
        ev_at_free = {qc: ev0 for qc in range(NQC)}
        sfree = [ev0, ev0]
        vfree = [ev0, ev0]
        ofree = [None, None]
        rd_free = [ev0, ev0]
        stt = {"bfree": None, "aob": ev0}
        ev_h = []
        pend = []

        def flush(item):
            qc, d2, c1, hh, bb = item
            gb = mm_group(p, psB_[0:64, :], [(ones64, rdb_[qc % 2])], waits=after(d2, e_o64, stt["bfree"]))
            a1 = vop(p, lambda E, qc=qc: E.tensor_tensor(aob, o64_[qc % 2], psB_[0:64, :], ALU.mult),
                     waits=after(c1, gb, stt["aob"]))
            stt["bfree"] = a1
            s_ = st(p, aoT[hh * 64:(hh + 1) * 64, bb * T + qc * 512: bb * T + (qc + 1) * 512], aob,
                    waits=after(a1))
            stt["aob"] = s_
            ofree[qc % 2] = a1
            rd_free[qc % 2] = gb
            ev_h.append(s_)

        for b_ in range(B):
            for h in range(16):
                lq = ld_f32(p, qh, qT[h * 64:(h + 1) * 64, b_ * T:(b_ + 1) * T], waits=after(ev_qk_free))
                lk = ld_f32(p, kh, kT[h * 64:(h + 1) * 64, b_ * T:(b_ + 1) * T], waits=after(ev_qk_free), q="scalar")
                lv = []
                for kt in range(NKT):
                    lv.append(ld_f32(p, vtb[:, kt * 65:kt * 65 + 64],
                                     vtok[b_ * T + kt * 128: b_ * T + (kt + 1) * 128, h * 64:(h + 1) * 64],
                                     waits=after(ev_av_free)))
                ev_l = after(lq, lk, *lv, ev_ones)
                ev_g_all = []
                ev_g2_all = []
                for qc in range(NQC):
                    exps = []
                    for kt in range(NKT):
                        bk = (qc * NKT + kt) % 2
                        g = mm_group(p, psS[bk], [(kh[:, kt * 128:(kt + 1) * 128], qh[:, qc * 512:(qc + 1) * 512])],
                                     waits=after(ev_l, sfree[bk]))
                        ev_g_all.append(g)
                        ee = aop(p, lambda E, bk=bk, kt=kt, qc=qc: E.activation(
                            at[:, kt * T + qc * 512: kt * T + (qc + 1) * 512], psS[bk], AF.Exp),
                            waits=after(g, ev_at_free[qc]))
                        sfree[bk] = ee
                        exps.append(ee)
                    bk = qc % 2
                    g2 = mm_group(p, psV[bk][0:65, :], [(vtb[:, kt * 65: kt * 65 + 65],
                                                at[:, kt * T + qc * 512: kt * T + (qc + 1) * 512])
                                               for kt in range(NKT)],
                                  waits=after(after(*exps), vfree[bk]))
                    ev_g2_all.append(g2)
                    ev_at_free[qc] = after(g2)
                    d1 = vop(p, lambda E, bk=bk, qc=qc: E.reciprocal(rden_[qc % 2], psV[bk][64:65, :]),
                             waits=after(g2, rd_free[qc % 2]))
                    d2 = vop(p, lambda E, qc=qc: E.tensor_copy(rdb_[qc % 2], rden_[qc % 2]), waits=after(d1))
                    c1 = aop(p, lambda E, bk=bk, qc=qc: E.activation(o64_[qc % 2], psV[bk][0:64, :], AF.Identity),
                             waits=after(g2, ofree[qc % 2]))
                    vfree[bk] = c1
                    pend.append((qc, d2, c1, h, b_))
                    if len(pend) > 1:
                        flush(pend.pop(0))
                ev_qk_free = after(*ev_g_all)
                ev_av_free = after(*ev_g2_all)
        while pend:
            flush(pend.pop(0))
        ev_ret = after(*ev_h)
    return ev_ret


def emit_s3c(nc, p, stack_outer, dio, B, T, ev0=(), pfx=''):
    TOK = B * T
    NCH = TOK // 512
    y2T = dio["y2T"]
    owT = dio["owT"]
    y3T = dio["y3T"]

    aoT = dio["aoT"]
    owb = dio["owb"]

    with ExitStack() as stack:
        e = stack.enter_context
        sb = lambda n, sh, dt=F32: e(nc.sbuf_tensor(pfx + n, sh, dt))[:]
        ww = sb("ww", [128, 8192], BF16)
        wb = sb("wb", [128, 8], F32)
        ab = sb("ab", [128, 4096], BF16)
        y2f = sb("y2f", [128, 4096], F32)
        sf = sb("sf", [128, 4096], F32)
        sqb = sb("sqb", [128, 4096], BF16)
        ybf = sb("ybf", [128, 4096], BF16)
        outf = sb("outf", [128, 4096], F32)
        rowA = sb("rowA", [1, 512], F32)
        rowB = sb("rowB", [1, 512], F32)
        rws = sb("rws", [1, 2048], F32)
        onec = sb("onec", [128, 1], BF16)
        onec32 = sb("onec32", [1, 128], F32)
        ps = [e(nc.psum_tensor(pfx + f"ps{i}", [128, 512], F32))[:] for i in range(4)]
        psS = e(nc.psum_tensor(pfx + "psS", [128, 512], F32))[:]
        psQ = e(nc.psum_tensor(pfx + "psQ", [128, 512], F32))[:]
        psA = e(nc.psum_tensor(pfx + "psA", [128, 512], F32))[:]
        psB = e(nc.psum_tensor(pfx + "psB", [128, 512], F32))[:]
        lw = [ld_bf16(p, ww[:, k * 1024:(k + 1) * 1024], owT[k * 128:(k + 1) * 128, :]) for k in range(8)]
        lb = ld_f32(p, wb, owb)
        e_one = gop(p, lambda E: E.memset(onec, 1.0))
        e_one32 = gop(p, lambda E: E.memset(onec32, 1.0))
        ev_w = after(*lw)

        bank_free = [ev0] * 4
        ev_stats_free = ev0
        ev_AB_free = None
        ev_chunk = ev0
        ev_st = None
        for c in range(NCH):
            la = [ld_f32(p, ab[:, k * 512:(k + 1) * 512],
                         aoT[k * 128:(k + 1) * 128, c * 512:(c + 1) * 512], waits=after(ev_chunk)) for k in range(8)]
            lyf = [ld_f32(p, y2f[:, k * 512:(k + 1) * 512],
                          y2T[k * 128:(k + 1) * 128, c * 512:(c + 1) * 512], waits=after(ev_chunk), q="scalar")
                   for k in range(8)]
            ev_la = after(*la)
            ev_s, sq_ev, cp_ev = [], [], []
            for m in range(8):
                bank = m % 4
                sl = slice(m * 512, (m + 1) * 512)
                g1 = mm_group(p, ps[bank], [(ww[:, k * 1024 + m * 128: k * 1024 + (m + 1) * 128],
                                             ab[:, k * 512:(k + 1) * 512]) for k in range(8)],
                              waits=after(ev_la, ev_w, bank_free[bank]))
                s1 = vop(p, lambda E, bank=bank, m=m, sl=sl: E.scalar_tensor_tensor(
                    sf[:, sl], ps[bank], wb[:, m:m + 1], y2f[:, sl], ALU.add, ALU.add),
                    waits=after(g1, lb, *lyf))
                bank_free[bank] = s1
                ev_s.append(s1)
                sq_ev.append(aop(p, lambda E, sl=sl: E.activation(sqb[:, sl], sf[:, sl], AF.Square),
                                 waits=after(s1)))
                cp_ev.append(vop(p, lambda E, sl=sl: E.tensor_copy(ybf[:, sl], sf[:, sl]), waits=after(s1)))
            ev_rows = emit_ln_stats(p, onec,
                                    [ybf[:, m * 512:(m + 1) * 512] for m in range(8)],
                                    [sqb[:, m * 512:(m + 1) * 512] for m in range(8)],
                                    psS, psQ, lambda i: rws[0:1, i * 512:(i + 1) * 512],
                                    rowA, rowB, 1e-5, after(after(*sq_ev), after(*cp_ev), e_one, ev_stats_free))
            ev_stats_free = ev_rows
            eA = emit_bcast_row(p, psA, onec32, rowA, after(ev_rows, e_one32, ev_AB_free))
            eB = emit_bcast_row(p, psB, onec32, rowB, after(ev_rows, e_one32, ev_AB_free))
            ev_f = []
            sts = []
            for m in range(8):
                sl = slice(m * 512, (m + 1) * 512)
                f1 = vop(p, lambda E, sl=sl: E.tensor_tensor(outf[:, sl], sf[:, sl], psA, ALU.mult),
                         waits=after(eA, ev_s[m], ev_st))
                f2 = vop(p, lambda E, sl=sl: E.tensor_tensor(outf[:, sl], outf[:, sl], psB, ALU.subtract),
                         waits=after(eB, f1))
                ev_f.append(f2)
                sts.append(st(p, y3T[m * 128:(m + 1) * 128, c * 512:(c + 1) * 512], outf[:, sl], waits=after(f2)))
            ev_AB_free = after(*ev_f)
            ev_st = after(*sts)
            ev_chunk = after(ev_st, ev_AB_free)
        ev_ret = after(ev_st, ev_chunk)
    return ev_ret


def emit_s4a(nc, p, stack_outer, dio, B, T, ev0=(), pfx=''):
    TOK = B * T
    NCH = TOK // 512
    y3T = dio["y3T"]
    idf = dio["idf"]

    WgT = dio["WgT"]
    mskT = dio["mskT"]

    with ExitStack() as stack:
        e = stack.enter_context
        sb = lambda n, sh, dt=F32: e(nc.sbuf_tensor(pfx + n, sh, dt))[:]
        wgb = sb("wgb", [128, 64], BF16)
        idb = sb("idb", [128, 128], F32)
        y3b = sb("y3b", [128, 4096], BF16)
        lgt = sb("lgt", [128, 8], F32)
        msk = sb("msk", [128, 8], F32)
        m1c = sb("m1c", [128, 1], F32)
        msk7 = sb("msk7", [128, 4], F32)
        col1 = sb("col1", [128, 1], F32)
        col2 = sb("col2", [128, 1], F32)
        runc = sb("runc", [128, 1], F32)
        rowt = sb("rowt", [1, 4096], F32)
        psL = e(nc.psum_tensor(pfx + "psL", [128, 512], F32))[:]
        pse = [e(nc.psum_tensor(pfx + f"pse{i}", [128, 512], F32))[:] for i in range(7)]
        lw = [ld_bf16(p, wgb[:, k * 8:(k + 1) * 8], WgT[k * 128:(k + 1) * 128, :]) for k in range(8)]
        lid = ld_f32(p, idb, idf)
        ev_w = after(*lw)
        ev_ev = ev0
        ev_msk_free = None
        ev_lg_free = ev0
        ev_y3_free = None
        ev_tr = {}
        for c in range(NCH):
            ly = [ld_bf16(p, y3b[:, k * 512:(k + 1) * 512],
                          y3T[k * 128:(k + 1) * 128, c * 512:(c + 1) * 512], waits=after(ev_y3_free))
                  for k in range(8)]
            ev_ly = after(*ly)
            trs_all = []
            for tt in range(4):
                g1 = mm_group(p, psL[:, 0:8], [(y3b[:, k * 512 + tt * 128: k * 512 + (tt + 1) * 128],
                                                wgb[:, k * 8:(k + 1) * 8]) for k in range(8)],
                              waits=after(ev_ly, ev_w, ev_lg_free, ev_ev))
                x0 = vop(p, lambda E: E.tensor_copy(lgt, psL[:, 0:8]), waits=after(g1, ev_lg_free))
                ev_lg_free = x0
                x1 = vop(p, lambda E: E.reduce_max(m1c, lgt, mybir.AxisListType.X), waits=after(x0, ev_msk_free))
                rz = vop(p, lambda E: E.memset(runc, 0.0), waits=after(ev_msk_free))
                for e_ in range(8):
                    q1 = vop(p, lambda E, e_=e_: E.tensor_tensor(col1, lgt[:, e_:e_ + 1], m1c, ALU.is_equal),
                             waits=after(x1, rz))
                    q2 = vop(p, lambda E: E.tensor_scalar(col2, runc, -1.0, 1.0, ALU.mult, ALU.add), waits=after(q1))
                    q3 = vop(p, lambda E, e_=e_: E.tensor_tensor(msk[:, e_:e_ + 1], col1, col2, ALU.mult),
                             waits=after(q2))
                    rz = vop(p, lambda E, e_=e_: E.tensor_tensor(runc, runc, msk[:, e_:e_ + 1], ALU.add),
                             waits=after(q3))
                trs = []
                for e_ in range(7):
                    dst = pse[e_][0:1, tt * 128:(tt + 1) * 128]
                    ev_tr[(e_, tt)] = p.op("tensor",
                                           lambda E, e_=e_, dst=dst: E.transpose(dst, msk[:, e_:e_ + 1], idb),
                                           waits=after(rz, lid, ev_ev), inc="pe")
                    trs.append(ev_tr[(e_, tt)])
                c7 = vop(p, lambda E, tt=tt: E.tensor_copy(msk7[:, tt:tt + 1], msk[:, 7:8]), waits=after(rz))
                ev_tr[(7, tt)] = c7
                trs.append(c7)
                ev_msk_free = after(*trs)
                trs_all += trs
            evs = []
            ee0 = None
            for e_ in range(7):
                ee = vop(p, lambda E, e_=e_: E.tensor_copy(
                    rowt[0:1, e_ * 512:(e_ + 1) * 512], pse[e_][0:1, 0:512]),
                    waits=after(*[ev_tr[(e_, tt)] for tt in range(4)]))
                if e_ == 0:
                    ee0 = ee
                evs.append(ee)
                evs.append(st(p, mskT[e_:e_ + 1, c * 512:(c + 1) * 512], rowt[0:1, e_ * 512:(e_ + 1) * 512],
                              waits=after(ee)))
            tr7 = []
            for tt in range(4):
                tr7.append(p.op("tensor", lambda E, tt=tt: E.transpose(
                    pse[0][0:1, tt * 128:(tt + 1) * 128], msk7[:, tt:tt + 1], idb),
                    waits=after(ee0, lid, *[ev_tr[(7, t_)] for t_ in range(4)]), inc="pe"))
            ee7 = vop(p, lambda E: E.tensor_copy(rowt[0:1, 7 * 512:8 * 512], pse[0][0:1, 0:512]),
                      waits=after(*tr7))
            evs.append(ee7)
            evs.append(st(p, mskT[7:8, c * 512:(c + 1) * 512], rowt[0:1, 7 * 512:8 * 512], waits=after(ee7)))
            ev_ev = after(*evs)
            trs_all += tr7
            ev_y3_free = after(*trs_all)
        ev_ret = after(ev_ev, ev_y3_free)
    return ev_ret


def emit_s4b(nc, p, stack_outer, dio, B, T, ev0=(), pfx=''):
    TOK = B * T
    NCH = TOK // 512
    y3T = dio["y3T"]
    W1 = dio["W1"]
    lnsc = dio["lnsc"]
    b1p = dio["b1p"]
    outT = dio["outT"]

    mskT = dio["mskT"]
    W2 = dio["W2"]
    lnb = dio["lnb"]
    b2m = dio["b2m"]

    with ExitStack() as stack:
        e = stack.enter_context
        sb = lambda n, sh, dt=F32: e(nc.sbuf_tensor(pfx + n, sh, dt))[:]
        w1b_ = sb("w1b_", [128, 16384], BF16)
        w2b_ = sb("w2b_", [128, 16384], BF16)
        b2b = sb("b2b", [8, 1024], BF16)
        lnscb = sb("lnscb", [128, 64], F32)
        lnbb = sb("lnbb", [128, 64], BF16)
        b1pb = sb("b1pb", [128, 16], F32)
        biaspc = sb("biaspc", [128, 16], F32)
        onesr1 = sb("onesr1", [1, 128], BF16)
        y3f = sb("y3f", [128, 4096], F32)
        xhatb = sb("xhatb", [128, 4096], BF16)
        sqb = sb("sqb", [128, 4096], BF16)
        ybf = sb("ybf", [128, 4096], BF16)
        mrows = sb("mrows", [8, 512], BF16)
        mrowb = sb("mrowb", [1, 4096], BF16)
        cMb = sb("cMb", [128, 512], BF16)
        gout = sb("gout", [128, 512], BF16)
        mh = sb("mh", [128, 16 * 512], BF16)
        sf = sb("sf", [128, 4096], F32)
        outf = sb("outf", [128, 4096], F32)
        rowA = sb("rowA", [1, 512], F32)
        rowB = sb("rowB", [1, 512], F32)
        rws = sb("rws", [1, 4096], F32)
        onec = sb("onec", [128, 1], BF16)
        onec32 = sb("onec32", [1, 128], F32)
        psk = [e(nc.psum_tensor(pfx + f"psk{i}", [128, 512], F32))[:] for i in range(8)]
        PS_S, PS_Q, PS_A, PS_B, PS_H0, PS_H1, PS_M, PS_O = range(8)

        lw = []
        for e_ in range(8):
            for k in range(8):
                lw.append(ld_bf16(p, w1b_[:, e_ * 2048 + k * 256: e_ * 2048 + (k + 1) * 256],
                                  W1[e_, k * 128:(k + 1) * 128, :]))
            for hk in range(2):
                lw.append(ld_bf16(p, w2b_[:, (e_ * 2 + hk) * 1024:(e_ * 2 + hk + 1) * 1024],
                                  W2[e_, hk * 128:(hk + 1) * 128, :]))
        lw.append(ld_bf16(p, b2b, b2m))
        lw.append(ld_bf16(p, lnbb, lnb))
        lv = [ld_f32(p, lnscb, lnsc), ld_f32(p, b1pb, b1p)]
        e_one = gop(p, lambda E: E.memset(onec, 1.0))
        e_one32 = gop(p, lambda E: E.memset(onec32, 1.0))
        e_or1 = gop(p, lambda E: E.memset(onesr1, 1.0))
        ev_w = after(*lw)
        ev_bp = []
        for e_ in range(8):
            for hf in range(2):
                g = mm_group(p, psk[PS_H0][:, 0:1],
                             [(w1b_[:, e_ * 2048 + k * 256 + hf * 128: e_ * 2048 + k * 256 + hf * 128 + 128],
                               lnbb[:, e_ * 8 + k: e_ * 8 + k + 1]) for k in range(8)],
                             waits=after(ev_w, *ev_bp[-1:]))
                ev_bp.append(aop(p, lambda E, e_=e_, hf=hf: E.activation(
                    biaspc[:, e_ * 2 + hf: e_ * 2 + hf + 1], psk[PS_H0][:, 0:1], AF.Identity,
                    bias=b1pb[:, e_ * 2 + hf: e_ * 2 + hf + 1]), waits=after(g, *lv)))
        ev_bpa = after(*ev_bp)
        ev_sc = []
        for e_ in range(8):
            for k in range(8):
                ev_sc.append(vop(p, lambda E, e_=e_, k=k: E.tensor_scalar_mul(
                    w1b_[:, e_ * 2048 + k * 256: e_ * 2048 + (k + 1) * 256],
                    w1b_[:, e_ * 2048 + k * 256: e_ * 2048 + (k + 1) * 256],
                    lnscb[:, e_ * 8 + k: e_ * 8 + k + 1]), waits=after(ev_bpa, *lv)))
        ev_w1s = after(*ev_sc)

        ev_chunk = ev0
        ev_st = None
        ev_stats_free = ev0
        ev_AB_free = None
        for c in range(NCH):
            lyf = [ld_f32(p, y3f[:, k * 512:(k + 1) * 512],
                          y3T[k * 128:(k + 1) * 128, c * 512:(c + 1) * 512], waits=after(ev_chunk)) for k in range(8)]
            lm = ld_bf16(p, mrows, mskT[:, c * 512:(c + 1) * 512], waits=after(ev_chunk))
            lme = [ld_bf16(p, mrowb[0:1, e_ * 512:(e_ + 1) * 512], mskT[e_:e_ + 1, c * 512:(c + 1) * 512],
                           waits=after(ev_chunk)) for e_ in range(8)]
            ev_ly = after(*lyf)
            sq_ev, cp_ev = [], []
            for m in range(8):
                sl = slice(m * 512, (m + 1) * 512)
                sq_ev.append(aop(p, lambda E, sl=sl: E.activation(sqb[:, sl], y3f[:, sl], AF.Square),
                                 waits=after(ev_ly, ev_chunk)))
                cp_ev.append(vop(p, lambda E, sl=sl: E.tensor_copy(ybf[:, sl], y3f[:, sl]),
                                 waits=after(ev_ly, ev_chunk)))
            ev_rows = emit_ln_stats(p, onec,
                                    [ybf[:, m * 512:(m + 1) * 512] for m in range(8)],
                                    [sqb[:, m * 512:(m + 1) * 512] for m in range(8)],
                                    psk[PS_S], psk[PS_Q], lambda i: rws[0:1, i * 512:(i + 1) * 512],
                                    rowA, rowB, 1e-5, after(after(*sq_ev), after(*cp_ev), e_one, ev_stats_free))
            ev_stats_free = ev_rows
            eA = emit_bcast_row(p, psk[PS_A], onec32, rowA, after(ev_rows, e_one32, ev_AB_free))
            eB = emit_bcast_row(p, psk[PS_B], onec32, rowB, after(ev_rows, e_one32, ev_AB_free))
            ev_xh = []
            for m in range(8):
                sl = slice(m * 512, (m + 1) * 512)
                x1 = vop(p, lambda E, sl=sl: E.tensor_tensor(sf[:, sl], y3f[:, sl], psk[PS_A], ALU.mult),
                         waits=after(eA, ev_ly))
                x2 = vop(p, lambda E, sl=sl: E.tensor_tensor(xhatb[:, sl], sf[:, sl], psk[PS_B], ALU.subtract),
                         waits=after(eB, x1))
                ev_xh.append(x2)
            ev_xha = after(*ev_xh)
            ev_AB_free = ev_xha
            hfree = [ev0, ev0]
            mfree = None
            gout_free = None
            ev_mh = []
            for e_ in range(8):
                gM = mm_group(p, psk[PS_M], [(onesr1, mrowb[0:1, e_ * 512:(e_ + 1) * 512])],
                              waits=after(*lme, e_or1, mfree))
                cm = vop(p, lambda E: E.tensor_copy(cMb, psk[PS_M]), waits=after(gM))
                mfree = cm
                for hf in range(2):
                    bk = hf
                    gH = mm_group(p, psk[PS_H0 + bk],
                                  [(w1b_[:, e_ * 2048 + k * 256 + hf * 128: e_ * 2048 + k * 256 + hf * 128 + 128],
                                    xhatb[:, k * 512:(k + 1) * 512]) for k in range(8)],
                                  waits=after(ev_xha, ev_w1s, hfree[bk]))
                    eg = aop(p, lambda E, bk=bk, e_=e_, hf=hf: E.activation(
                        gout, psk[PS_H0 + bk], AF.Gelu, bias=biaspc[:, e_ * 2 + hf: e_ * 2 + hf + 1]),
                        waits=after(gH, ev_bpa, gout_free))
                    hfree[bk] = eg
                    em = vop(p, lambda E, e_=e_, hf=hf: E.tensor_tensor(
                        mh[:, (e_ * 2 + hf) * 512:(e_ * 2 + hf + 1) * 512], gout, cMb, ALU.mult),
                        waits=after(eg, cm))
                    gout_free = em
                    ev_mh.append(em)
            ev_mha = after(*ev_mh)
            ofree = [ev0, ev0]
            ev_s = []
            for dm in range(8):
                bk = dm % 2
                pairs = [(w2b_[:, (e_ * 2 + hk) * 1024 + dm * 128:(e_ * 2 + hk) * 1024 + (dm + 1) * 128],
                          mh[:, (e_ * 2 + hk) * 512:(e_ * 2 + hk + 1) * 512])
                         for e_ in range(8) for hk in range(2)]
                pairs.append((b2b[:, dm * 128:(dm + 1) * 128], mrows))
                gO = mm_group(p, psk[PS_O - bk], pairs, waits=after(ev_mha, ev_w, lm, ofree[bk]))
                s1 = vop(p, lambda E, bk=bk, dm=dm: E.tensor_tensor(
                    sf[:, dm * 512:(dm + 1) * 512], psk[PS_O - bk], y3f[:, dm * 512:(dm + 1) * 512], ALU.add),
                    waits=after(gO, ev_ly))
                ofree[bk] = s1
                ev_s.append(s1)
            sq_ev2, cp_ev2 = [], []
            for m in range(8):
                sl = slice(m * 512, (m + 1) * 512)
                sq_ev2.append(aop(p, lambda E, sl=sl: E.activation(sqb[:, sl], sf[:, sl], AF.Square),
                                  waits=after(ev_s[m])))
                cp_ev2.append(vop(p, lambda E, sl=sl: E.tensor_copy(ybf[:, sl], sf[:, sl]), waits=after(ev_s[m])))
            ev_rows2 = emit_ln_stats(p, onec,
                                     [ybf[:, m * 512:(m + 1) * 512] for m in range(8)],
                                     [sqb[:, m * 512:(m + 1) * 512] for m in range(8)],
                                     psk[PS_S], psk[PS_Q], lambda i: rws[0:1, 2048 + i * 512:2048 + (i + 1) * 512],
                                     rowA, rowB, 1e-5, after(after(*sq_ev2), after(*cp_ev2), ev_rows))
            ev_stats_free = ev_rows2
            eA2 = emit_bcast_row(p, psk[PS_A], onec32, rowA, after(ev_rows2, ev_xha))
            eB2 = emit_bcast_row(p, psk[PS_B], onec32, rowB, after(ev_rows2, ev_xha))
            ev_f = []
            sts = []
            for m in range(8):
                sl = slice(m * 512, (m + 1) * 512)
                f1 = vop(p, lambda E, sl=sl: E.tensor_tensor(outf[:, sl], sf[:, sl], psk[PS_A], ALU.mult),
                         waits=after(eA2, ev_s[m], ev_st))
                f2 = vop(p, lambda E, sl=sl: E.tensor_tensor(outf[:, sl], outf[:, sl], psk[PS_B], ALU.subtract),
                         waits=after(eB2, f1))
                ev_f.append(f2)
                sts.append(st(p, outT[m * 128:(m + 1) * 128, c * 512:(c + 1) * 512], outf[:, sl], waits=after(f2)))
            ev_AB_free = after(*ev_f)
            ev_st = after(*sts)
            ev_chunk = after(ev_st, ev_AB_free, ev_mha)
        ev_ret = after(ev_st, ev_chunk)
    return ev_ret


def _decls(B, T):
    TOK = B * T
    return {
        "s1a": dict(ins=[("xT", [1024, TOK], F32), ("w_in", [3, 1024, 1024], F32), ("b_in", [128, 24], F32),
                         ("pl", [128, 24], F32), ("w_mid", [2, 1024, 1024], F32), ("b_mid", [128, 16], F32)],
                    outs=[("mid", [1024, TOK], F32)]),
        "s1b": dict(ins=[("xT", [1024, TOK], F32), ("midT", [1024, TOK], F32), ("w1w", [1024, 1024], F32),
                         ("w2w", [1024, 1024], F32), ("vecs", [128, 32], F32)],
                    outs=[("y1", [1024, TOK], F32)]),
        "s2a": dict(ins=[("y1T", [1024, TOK], F32), ("wihT", [1024, 3072], F32), ("exprT", [1024, 1024], F32),
                         ("ctxT", [1024, 1024], F32), ("ctxb", [128, 8], F32), ("idf", [128, 128], F32)],
                    outs=[("xw_tok", [TOK, 3072], F32), ("ex_tok", [TOK, 1024], F32), ("h0T", [128, 32], F32),
                          ("h0r", [4, 1024], F32)]),
        "s2b": dict(ins=[("xw_tok", [TOK, 3072], F32), ("whhT", [1024, 3072], F32), ("h0T", [128, 32], F32),
                         ("h0r", [4, 1024], F32), ("i4", [4, 4], F32)],
                    outs=[("hs_tok", [TOK, 1024], F32), ("hTout", [128, 32], F32), ("hrout", [4, 1024], F32)]),
        "s2c": dict(ins=[("hs_tok", [TOK, 1024], F32), ("ex_tok", [TOK, 1024], F32)],
                    outs=[("selv", [TOK, 24], F32), ("penp", [1, 8], F32)]),
        "ccpen": dict(ins=[("penp", [1, 8], F32)], outs=[("pensum", [1, 8], F32)]),
        "s2d0": dict(ins=[("selv", [TOK, 24], F32), ("pensum", [1, 8], F32), ("idf", [128, 128], F32)],
                     outs=[("cwT", [8, TOK], F32)]),
        "s2d": dict(ins=[("y1T", [1024, TOK], F32), ("cwT", [8, TOK], F32), ("gT", [8, 1024, 2048], BF16),
                         ("uT", [8, 1024, 2048], BF16), ("dT", [8, 2048, 1024], BF16), ("sgT", [1024, 2048], BF16),
                         ("suT", [1024, 2048], BF16), ("sdT", [2048, 1024], BF16)],
                    outs=[("accT", [1024, TOK], F32)]),
        "s2e": dict(ins=[("y1T", [1024, TOK], F32), ("accT", [1024, TOK], F32)],
                    outs=[("y2T", [1024, TOK], F32)]),
        "s3a": dict(ins=[("y2T", [1024, TOK], F32), ("qkvT", [1024, 3072], F32), ("qkvb", [128, 24], F32),
                         ("ropeC", [128, T], F32), ("ropeS", [128, T], F32), ("sel2e", [128, 4], F32),
                         ("exp2", [2, 128], F32), ("idf", [128, 128], F32)],
                    outs=[("qTo", [1024, TOK], BF16), ("kTo", [1024, TOK], BF16), ("vtok", [TOK, 1024], BF16)]),
        "s3b": dict(ins=[("qT", [1024, TOK], BF16), ("kT", [1024, TOK], BF16), ("vtok", [TOK, 1024], BF16)],
                    outs=[("aoT", [1024, TOK], BF16)]),
        "s3c": dict(ins=[("y2T", [1024, TOK], F32), ("aoT", [1024, TOK], BF16), ("owT", [1024, 1024], F32),
                         ("owb", [128, 8], F32)],
                    outs=[("y3T", [1024, TOK], F32)]),
        "s4a": dict(ins=[("y3T", [1024, TOK], F32), ("WgT", [1024, 8], F32), ("idf", [128, 128], F32)],
                    outs=[("mskT", [8, TOK], F32)]),
        "s4b": dict(ins=[("y3T", [1024, TOK], F32), ("mskT", [8, TOK], F32), ("W1", [8, 1024, 256], F32),
                         ("W2", [8, 256, 1024], F32), ("lnsc", [128, 64], F32), ("lnb", [128, 64], F32),
                         ("b1p", [128, 16], F32), ("b2m", [8, 1024], F32)],
                    outs=[("outT", [1024, TOK], F32)]),
    }


_EMITS = {}


def _register_emits():
    _EMITS.update(s1a=emit_s1a, s1b=emit_s1b, s2a=emit_s2a, s2b=emit_s2b, s2c=emit_s2c,
                  ccpen=emit_ccpen, s2d0=emit_s2d0, s2d=emit_s2d, s2e=emit_s2e,
                  s3a=emit_s3a, s3b=emit_s3b, s3c=emit_s3c, s4a=emit_s4a, s4b=emit_s4b)


def build_stage(name, B, T, **kw):
    _register_emits()
    d = _decls(B, T)[name]
    nc = bass.Bass("TRN2", target_bir_lowering=False)
    dio = {}
    for n, sh, dt in d["ins"]:
        dio[n] = nc.dram_tensor(n, sh, dt, kind="ExternalInput").ap()
    for n, sh, dt in d["outs"]:
        dio[n] = nc.dram_tensor(n, sh, dt, kind="ExternalOutput").ap()
    with ExitStack() as stack:
        block = stack.enter_context(nc.Block())
        p = Prog()
        ev = _EMITS[name](nc, p, stack, dio, B, T, **kw)
        fence(p, "sync", after(ev))
        p.emit(nc, block, stack)
    return nc


def barrier(p):
    evs = [p.op(eng, lambda E: E.nop(), inc="bar") for eng in Prog.ENG]
    return after(*evs)


_FUSED_A = ("s1a", "s1b", "s2a", "s2b", "s2c")
_FUSED_B = ("s2d0", "s2d", "s2e", "s3a", "s3b", "s3c", "s4a", "s4b")
_ALIAS = {
    "s1b": {"midT": "mid"},
    "s2a": {"y1T": "y1"},
    "s2d": {"y1T": "y1"},
    "s2e": {"y1T": "y1"},
    "s3b": {"qT": "qTo", "kT": "kTo"},
}


def build_fused(stages, B, T, externals_in, externals_out, **stage_kw):
    _register_emits()
    ds = _decls(B, T)
    nc = bass.Bass("TRN2", target_bir_lowering=False)
    reg = {}

    def get(gname, sh, dt):
        if gname not in reg:
            if gname in externals_in:
                kind = "ExternalInput"
            elif gname in externals_out:
                kind = "ExternalOutput"
            else:
                kind = "Internal"
            reg[gname] = nc.dram_tensor(gname, sh, dt, kind=kind).ap()
        return reg[gname]

    with ExitStack() as stack:
        block = stack.enter_context(nc.Block())
        p = Prog()
        ev0 = ()
        for si, name in enumerate(stages):
            d = ds[name]
            al = _ALIAS.get(name, {})
            dio = {}
            for n, sh, dt in d["ins"] + d["outs"]:
                dio[n] = get(al.get(n, n), sh, dt)
            kw = dict(stage_kw.get(name, {}))
            p.ambient = tuple(ev0)
            ev = _EMITS[name](nc, p, stack, dio, B, T, ev0=ev0, pfx=name + "_", **kw)
            p.ambient = ()
            bar = barrier(p)
            ev0 = after(ev, bar)
        fence(p, "sync", after(ev0))
        p.emit(nc, block, stack)
    return nc




def build_s1a(B, T):
    return build_stage("s1a", B, T)


def build_s1b(B, T):
    return build_stage("s1b", B, T)


def build_s2a(B, T):
    return build_stage("s2a", B, T)


def build_s2b(B, T, **kw):
    return build_stage("s2b", B, T, **kw)


def build_s2c(B, T, total_tokens=65536):
    return build_stage("s2c", B, T, total_tokens=total_tokens)


def build_s2d0(B, T):
    return build_stage("s2d0", B, T)


def build_s2d(B, T, nexp=9):
    return build_stage("s2d", B, T, nexp=nexp)


def build_s2e(B, T):
    return build_stage("s2e", B, T)


def build_s3a(B, T):
    return build_stage("s3a", B, T)


def build_s3b(B, T):
    return build_stage("s3b", B, T)


def build_s3c(B, T):
    return build_stage("s3c", B, T)


def build_s4a(B, T):
    return build_stage("s4a", B, T)


def build_s4b(B, T):
    return build_stage("s4b", B, T)


_FUSED_ALL = _FUSED_A + ("ccpen",) + _FUSED_B


def build_fusedAll(B, T, total_tokens=65536):
    return build_fused(_FUSED_ALL, B, T,
                       externals_in={"xT", "w_in", "b_in", "pl", "w_mid", "b_mid", "w1w", "w2w", "vecs",
                                     "wihT", "exprT", "ctxT", "ctxb", "idf", "whhT", "i4",
                                     "gT", "uT", "dT", "sgT", "suT", "sdT",
                                     "qkvT", "qkvb", "ropeC", "ropeS", "sel2e", "exp2", "owT", "owb",
                                     "WgT", "W1", "W2", "lnsc", "lnb", "b1p", "b2m"},
                       externals_out={"outT"},
                       **dict(s2c=dict(total_tokens=total_tokens)))


def build_fusedA(B, T, total_tokens=65536):
    return build_fused(_FUSED_A, B, T,
                       externals_in={"xT", "w_in", "b_in", "pl", "w_mid", "b_mid", "w1w", "w2w", "vecs",
                                     "wihT", "exprT", "ctxT", "ctxb", "idf", "whhT", "i4"},
                       externals_out={"y1", "selv", "penp"},
                       **{"s2c": {}} if False else dict(s2c=dict(total_tokens=total_tokens)))


def build_fusedB(B, T):
    return build_fused(_FUSED_B, B, T,
                       externals_in={"y1", "selv", "pen8", "idf", "gT", "uT", "dT", "sgT", "suT", "sdT",
                                     "qkvT", "qkvb", "ropeC", "ropeS", "sel2e", "exp2", "owT", "owb",
                                     "WgT", "W1", "W2", "lnsc", "lnb", "b1p", "b2m"},
                       externals_out={"outT"})


def pack_cols(*vs):
    outs = []
    for v in vs:
        v = np.asarray(v, dtype=np.float32).reshape(-1)
        m = v.shape[0] // 128
        outs.append(v.reshape(m, 128).T)
    return np.ascontiguousarray(np.concatenate(outs, axis=1))


def to_T(x):
    x = np.asarray(x, dtype=np.float32)
    return np.ascontiguousarray(x.reshape(-1, x.shape[-1]).T)


def from_T(xT, B, T):
    return np.ascontiguousarray(np.asarray(xT, dtype=np.float32).T.reshape(B, T, -1))


def s1_inputs(x_shard, inp):
    m1 = dict(
        xT=to_T(x_shard),
        w_in=np.ascontiguousarray(np.asarray(inp["L0_attn_W_in"], dtype=np.float32)),
        b_in=pack_cols(inp["L0_attn_b_in"][0], inp["L0_attn_b_in"][1], inp["L0_attn_b_in"][2]),
        pl=pack_cols(inp["L0_attn_params_log"][0], inp["L0_attn_params_log"][1], inp["L0_attn_params_log"][2]),
        w_mid=np.ascontiguousarray(np.asarray(inp["L0_attn_W_mid"], dtype=np.float32)),
        b_mid=pack_cols(inp["L0_attn_b_mid"][0], inp["L0_attn_b_mid"][1]),
    )
    m2w = dict(
        w1w=np.ascontiguousarray(np.asarray(inp["L0_attn_w1_w"], dtype=np.float32)),
        w2w=np.ascontiguousarray(np.asarray(inp["L0_attn_w2_w"], dtype=np.float32)),
        vecs=pack_cols(inp["L0_attn_w1_b"], inp["L0_attn_w2_b"], inp["L0_attn_ln_g"], inp["L0_attn_ln_b"]),
    )
    return m1, m2w


def _qk_order():
    o = np.zeros(1024, dtype=np.int64)
    for h in range(16):
        for i in range(32):
            o[h * 64 + i] = h * 64 + 2 * i
            o[h * 64 + 32 + i] = h * 64 + 2 * i + 1
    return o


_BUILD_CACHE = {}


def _get(name, fn):
    if name not in _BUILD_CACHE:
        _BUILD_CACHE[name] = fn()
    return _BUILD_CACHE[name]


def _run(name, fn, maps):
    nc = _get(name, fn)
    return run_bass_kernel_spmd(nc, maps, core_ids=list(range(NCORES))).results


def host_maps(inp, B, T, x_shards):
    f32 = lambda a: np.ascontiguousarray(np.asarray(a, dtype=np.float32))
    idf = np.eye(128, dtype=np.float32)
    ordq = _qk_order()
    order = np.concatenate([ordq, 1024 + ordq, 2048 + np.arange(1024)])
    HD2 = 32
    fr = np.arange(HD2, dtype=np.float64)
    inv = 1.0 / 10000 ** (fr / HD2)
    pos = np.arange(T, dtype=np.float64)
    th = pos[None, :] * inv[(np.arange(128) % 32)][:, None]
    sel2e = np.zeros((128, 4), np.float32)
    sel2e[0:32, 0] = 1; sel2e[64:96, 1] = 1; sel2e[32:64, 2] = 1; sel2e[96:128, 3] = 1
    exp2 = np.zeros((2, 128), np.float32)
    exp2[0, 0:64] = 1; exp2[1, 64:128] = 1
    sharedA = dict(
        w_in=f32(inp["L0_attn_W_in"]),
        b_in=pack_cols(inp["L0_attn_b_in"][0], inp["L0_attn_b_in"][1], inp["L0_attn_b_in"][2]),
        pl=pack_cols(inp["L0_attn_params_log"][0], inp["L0_attn_params_log"][1], inp["L0_attn_params_log"][2]),
        w_mid=f32(inp["L0_attn_W_mid"]),
        b_mid=pack_cols(inp["L0_attn_b_mid"][0], inp["L0_attn_b_mid"][1]),
        w1w=f32(inp["L0_attn_w1_w"]), w2w=f32(inp["L0_attn_w2_w"]),
        vecs=pack_cols(inp["L0_attn_w1_b"], inp["L0_attn_w2_b"], inp["L0_attn_ln_g"], inp["L0_attn_ln_b"]),
        wihT=f32(inp["L0_ffn_gru_w_ih"].T), exprT=f32(inp["L0_ffn_expr_w"].T),
        ctxT=f32(inp["L0_ffn_ctx_w"].T), ctxb=pack_cols(inp["L0_ffn_ctx_b"]),
        whhT=f32(inp["L0_ffn_gru_w_hh"].T), i4=np.eye(4, dtype=np.float32), idf=idf,
    )
    import ml_dtypes
    bf16 = lambda a: np.ascontiguousarray(np.asarray(a, dtype=np.float32).astype(ml_dtypes.bfloat16))
    sharedB = dict(
        idf=idf,
        gT=bf16(inp["L0_ffn_gate_w"].transpose(0, 2, 1)),
        uT=bf16(inp["L0_ffn_up_w"].transpose(0, 2, 1)),
        dT=bf16(inp["L0_ffn_down_w"].transpose(0, 2, 1)),
        sgT=bf16(inp["L0_ffn_shared_gate_w"].T), suT=bf16(inp["L0_ffn_shared_up_w"].T),
        sdT=bf16(inp["L0_ffn_shared_down_w"].T),
        qkvT=f32(inp["L1_attn_qkv_w"].T[:, order]),
        qkvb=pack_cols(np.asarray(inp["L1_attn_qkv_b"], np.float32)[order]),
        ropeC=f32(np.cos(th)), ropeS=f32(np.sin(th)), sel2e=sel2e, exp2=exp2,
        owT=f32(inp["L1_attn_out_w"].T), owb=pack_cols(inp["L1_attn_out_b"]),
        WgT=f32(inp["L1_ffn_Wg"].T),
        W1=f32(inp["L1_ffn_W1"]), W2=f32(inp["L1_ffn_W2"]),
        lnsc=f32(np.asarray(inp["L1_ffn_ln_scale"], np.float32).reshape(8, 8, 128).transpose(2, 0, 1).reshape(128, 64)),
        lnb=f32(np.asarray(inp["L1_ffn_ln_bias"], np.float32).reshape(8, 8, 128).transpose(2, 0, 1).reshape(128, 64)),
        b1p=f32(np.asarray(inp["L1_ffn_b1"], np.float32).reshape(8, 2, 128).transpose(2, 0, 1).reshape(128, 16)),
        b2m=f32(inp["L1_ffn_b2"]),
    )
    mapsA = [dict(sharedA, xT=to_T(xs)) for xs in x_shards]
    mapsB = [dict(sharedB) for _ in x_shards]
    return mapsA, mapsB


def kernel(**inputs):
    inp = {k: np.asarray(v) for k, v in inputs.items()}
    Bfull, T, D = inp["L0_attn_x"].shape
    B = Bfull // NCORES
    x_shards = [inp["L0_attn_x"][c * B:(c + 1) * B] for c in range(NCORES)]
    mapsA, mapsB = host_maps(inp, B, T, x_shards)
    maps = [dict(mapsA[c], **mapsB[c]) for c in range(NCORES)]
    nc = _get("fusedAll", lambda: build_fusedAll(B, T, total_tokens=Bfull * T))
    r = run_bass_kernel_spmd(nc, maps, core_ids=list(range(NCORES))).results
    out = [from_T(np.asarray(r[c]["outT"], np.float32), B, T) for c in range(NCORES)]
    return np.ascontiguousarray(np.concatenate(out, axis=0))
